# version 21
# baseline (speedup 1.0000x reference)
import numpy as np
import ml_dtypes
from contextlib import ExitStack

import concourse.bass as bass
import concourse.mybir as mybir
from concourse.bass_utils import run_bass_kernel_spmd
from concourse.bass_core_rust import IndirectOffsetOnAxis

F32 = mybir.dt.float32
BF16 = mybir.dt.bfloat16
I32 = mybir.dt.int32
AF = mybir.ActivationFunctionType
ALU = mybir.AluOpType
AX = mybir.AxisListType
AP = bass.AP

D = 1024
T = 2048
NH = 16
HD = 64
E0 = 16
C0 = 2560
CL = 448
NSLOT = E0 * CL
F0 = 1408
E1 = 8
C1 = 2048
EPS = 1e-5
NT = D // 128
NC = T // 128
NF = F0 // 128


_RW = {
    "matmul": ((0,), (1, 2)),
    "transpose": ((0,), (1, 2)),
    "activation": ((0,), (1,)),
    "tensor_copy": ((0,), (1,)),
    "tensor_tensor": ((0,), (1, 2)),
    "tensor_add": ((0,), (1, 2)),
    "tensor_sub": ((0,), (1, 2)),
    "tensor_mul": ((0,), (1, 2)),
    "tensor_max": ((0,), (1, 2)),
    "tensor_scalar": ((0,), (1, 2, 3)),
    "tensor_single_scalar": ((0,), (1, 2)),
    "tensor_scalar_add": ((0,), (1, 2)),
    "tensor_scalar_sub": ((0,), (1, 2)),
    "tensor_scalar_mul": ((0,), (1, 2)),
    "tensor_scalar_max": ((0,), (1, 2)),
    "tensor_scalar_min": ((0,), (1, 2)),
    "scalar_tensor_tensor": ((0,), (1, 2, 3)),
    "tensor_tensor_scan": ((0,), (1, 2, 3)),
    "tensor_reduce": ((0,), (1,)),
    "reduce_sum": ((0,), (1,)),
    "reduce_max": ((0,), (1,)),
    "reciprocal": ((0,), (1,)),
    "select": ((0,), (1, 2, 3)),
    "max": ((), ()),
    "memset": ((0,), ()),
    "memzero": ((0,), ()),
    "iota": ((0,), ()),
    "affine_select": ((0,), (1,)),
    "dma_start": ((), ()),
    "indirect_dma_start": ((0, 1), (2, 3)),
}
_DMA = {"dma_start", "indirect_dma_start"}
_WKW = {"out", "accum_out", "out_max", "out_indices", "out_offset"}


def _aps_of(x):
    if isinstance(x, AP):
        return [x]
    if isinstance(x, IndirectOffsetOnAxis):
        return [x.ap]
    return []


def _tn(ap):
    t = ap.tensor
    return getattr(t, "name", None) or str(id(t))


class _EngProxy:
    def __init__(self, rec, eng):
        self._rec = rec
        self._eng = eng

    def __getattr__(self, name):
        def call(*args, **kwargs):
            return self._rec._record(self._eng, name, args, kwargs)
        return call


class Rec:
    ENGS = ("t", "v", "s", "g", "q")

    def __init__(self, nc):
        self.nc = nc
        self.ops = []
        self.w_last = {}
        self.r_last = {}
        for e in self.ENGS:
            setattr(self, e, _EngProxy(self, e))

    def _record(self, eng, name, args, kwargs):
        assert name in _RW, f"unknown method {name}"
        outp, inp = _RW[name]
        reads, writes = [], []
        for i, a in enumerate(args):
            for ap in _aps_of(a):
                (writes if i in outp else reads).append(_tn(ap))
        for kw, val in kwargs.items():
            for ap in _aps_of(val):
                (writes if kw in _WKW else reads).append(_tn(ap))
        idx = len(self.ops)
        deps = set(getattr(self, "_fence", ()))
        sdeps = set()
        is_dma = name in _DMA
        soft = eng == "t" and not is_dma
        for tn in reads:
            for e2, w in self.w_last.get(tn, {}).items():
                if soft and e2 == eng and not self.ops[w]["dma"]:
                    sdeps.add(w)
                else:
                    deps.add(w)
        for tn in writes:
            for e2, w in self.w_last.get(tn, {}).items():
                if soft and e2 == eng and not self.ops[w]["dma"]:
                    sdeps.add(w)
                else:
                    deps.add(w)
            for e2, rr in self.r_last.get(tn, {}).items():
                if soft and e2 == eng and not self.ops[rr]["dma"]:
                    sdeps.add(rr)
                else:
                    deps.add(rr)
        self.ops.append(dict(eng=eng, name=name, args=args, kwargs=kwargs,
                             deps=deps, sdeps=sdeps, dma=name in _DMA, sig=False))
        for tn in writes:
            self.w_last.setdefault(tn, {})[eng] = idx
        for tn in reads:
            self.r_last.setdefault(tn, {})[eng] = idx
        return idx

    def emit(self, ctx):
        nc = self.nc
        for op in self.ops:
            if op["dma"]:
                op["sig"] = True
            for d in op["deps"]:
                self.ops[d]["sig"] = True
        cnt = {e: 0 for e in self.ENGS}
        for op in self.ops:
            if op["sig"]:
                cnt[op["eng"]] += 16 if op["dma"] else 1
                op["semval"] = cnt[op["eng"]]
        totals = dict(cnt)
        sems = {e: ctx.enter_context(nc.semaphore("sem_" + e)) for e in self.ENGS}
        ops = self.ops
        block = ctx.enter_context(nc.Block())

        def runner(eng_name):
            def body(engine):
                ndma = 0
                waited = {}
                for op in ops:
                    if op["eng"] != eng_name:
                        continue
                    best = {}
                    if op["dma"] and ndma > 0:
                        best[eng_name] = 16 * ndma
                    for d in op["deps"]:
                        dop = ops[d]
                        best[dop["eng"]] = max(best.get(dop["eng"], 0), dop["semval"])
                    for e2, vv in best.items():
                        if waited.get(e2, 0) < vv:
                            engine.wait_ge(sems[e2], vv)
                            waited[e2] = vv
                    inst = getattr(engine, op["name"])(*op["args"], **op["kwargs"])
                    if op["sig"]:
                        inst.then_inc(sems[eng_name], 16 if op["dma"] else 1)
                    if op["dma"]:
                        ndma += 1
                if eng_name in ("q", "g") and totals[eng_name] > 0:
                    engine.wait_ge(sems[eng_name], totals[eng_name])
            return body

        block.sync(runner("q"))
        block.vector(runner("v"))
        block.scalar(runner("s"))
        block.gpsimd(runner("g"))
        block.tensor(runner("t"))

    def fence(self):
        last = {}
        for i, op in enumerate(self.ops):
            last[op["eng"]] = i
        self._fence = set(last.values())

    def bcast(self, psum_ap, ones_row, row_ap):
        w = row_ap.shape[-1]
        for o in range(0, w, 512):
            ww = min(512, w - o)
            self.t.matmul(psum_ap[:, o:o + ww], ones_row, row_ap[:, o:o + ww],
                          start=True, stop=True)



def build_A():
    nc = bass.Bass("TRN2", target_bir_lowering=False, detect_race_conditions=False)
    dtr = nc.dram_tensor
    I = lambda n, s, d=F32: dtr(n, s, d, kind="ExternalInput").ap()
    O = lambda n, s, d=F32: dtr(n, s, d, kind="ExternalOutput").ap()

    xT = I("xT", [D, T])
    Wq = I("Wq", [D, D]); Wk = I("Wk", [D, D]); Wv = I("Wv", [D, D])
    WoR = I("WoR", [HD, NH * D])
    bqR = I("bqR", [HD, NH]); bkR = I("bkR", [HD, NH])
    bvB = I("bvB", [128, D])
    bo = I("bo", [D, 1]); lnw = I("lnw", [D, 1]); lnb = I("lnb", [D, 1])
    rw = I("rw", [D, E0]); rb = I("rb", [E0, 1])
    swg = I("swg", [D, F0]); swu = I("swu", [D, F0]); swd = I("swd", [F0, D])
    onesr = I("onesr", [1, 128])
    onescf = I("onescf", [128, 1])
    onesbc = I("onesbc", [128, 1], BF16)
    iota16c = I("iota16c", [E0, 1])
    iota16r = I("iota16r", [128, E0])
    ident16 = I("ident16", [E0, E0])
    id128 = I("id128", [128, 128])

    oT_o = O("oT", [D, T], BF16)
    shT_o = O("shT", [D, T], BF16)
    xnbf_o = O("xnbf", [D, T], BF16)
    rows_o = O("rows", [1, 8 * T])
    cnt_o = O("cnt", [E0, 1])

    r = Rec(nc)
    q, g, v, s, t = r.q, r.g, r.v, r.s, r.t
    ctx = ExitStack()
    e = ctx.enter_context

    ar = e(nc.sbuf_tensor("arena", [1, 4 * T], F32))
    A0 = ar[0:1, 0:T]; A1 = ar[0:1, T:2 * T]; A2 = ar[0:1, 2 * T:3 * T]; A3 = ar[0:1, 3 * T:4 * T]
    stmp0 = e(nc.sbuf_tensor("stmp0", [128, 1024], F32))
    sones = e(nc.sbuf_tensor("sones", [1, 128], F32))
    sonescf = e(nc.sbuf_tensor("sonescf", [128, 1], F32))
    sonesbc = e(nc.sbuf_tensor("sonesbc", [128, 1], BF16))
    siota16c = e(nc.sbuf_tensor("siota16c", [E0, 1], F32))
    siota16r = e(nc.sbuf_tensor("siota16r", [128, E0], F32))
    sident16 = e(nc.sbuf_tensor("sident16", [E0, E0], F32))
    sid128 = e(nc.sbuf_tensor("sid128", [128, 128], F32))
    sbias = e(nc.sbuf_tensor("sbias", [128, NT, 3], F32))
    srb = e(nc.sbuf_tensor("srb", [E0, 1], F32))
    scsum = e(nc.sbuf_tensor("scsum", [E0, 1], F32))
    scnt = e(nc.sbuf_tensor("scnt", [E0, 1], F32))
    szer16 = e(nc.sbuf_tensor("szer16", [128, E0], F32))

    P0 = e(nc.psum_tensor("P0", [128, 512], F32))
    P1 = e(nc.psum_tensor("P1", [128, 512], F32))
    PST = e(nc.psum_tensor("PST", [128, 1024], F32))
    PE2 = e(nc.psum_tensor("PE2", [128, 1024], F32))
    PF = e(nc.psum_tensor("PF", [128, 1024], F32))
    BK6 = [P0[:, 0:512], P1[:, 0:512], PE2[:, 0:512], PE2[:, 512:1024],
           PF[:, 0:512], PF[:, 512:1024]]

    q.dma_start(out=sones[:], in_=onesr[:, :])
    q.dma_start(out=sonescf[:], in_=onescf[:, :])
    q.dma_start(out=sonesbc[:], in_=onesbc[:, :])
    q.dma_start(out=siota16c[:], in_=iota16c[:, :])
    q.dma_start(out=siota16r[:], in_=iota16r[:, :])
    q.dma_start(out=sident16[:], in_=ident16[:, :])
    q.dma_start(out=sid128[:], in_=id128[:, :])
    q.dma_start(out=srb[:], in_=rb[:, :])
    for a in range(NT):
        q.dma_start(out=sbias[:, a, 0:1], in_=lnw[a * 128:(a + 1) * 128, :])
        q.dma_start(out=sbias[:, a, 1:2], in_=lnb[a * 128:(a + 1) * 128, :])
        q.dma_start(out=sbias[:, a, 2:3], in_=bo[a * 128:(a + 1) * 128, :])
    v.memset(szer16[:], 0.0)

    sxnbf_scope = ExitStack()
    sxnbf = sxnbf_scope.enter_context(nc.sbuf_tensor("sxnbf", [128, NT, T], BF16))

    with ExitStack() as ph:
        e1 = ph.enter_context
        slog = e1(nc.sbuf_tensor("slog", [E0, T], F32))
        slogtm = e1(nc.sbuf_tensor("slogtm", [128, NC, E0], F32))
        sw16 = e1(nc.sbuf_tensor("sw16", [128, NC, 12], F32))
        spk = e1(nc.sbuf_tensor("spk", [128, NC, 4], F32))
        stk = e1(nc.sbuf_tensor("stk", [128, 48], F32))
        ohA = e1(nc.sbuf_tensor("ohA", [E0, T], F32))
        ohB = e1(nc.sbuf_tensor("ohB", [E0, T], F32))
        ohC = e1(nc.sbuf_tensor("ohC", [E0, T], F32))
        str4 = e1(nc.sbuf_tensor("str4", [4, T], F32))

        g1 = ExitStack()
        sxTp = [g1.enter_context(nc.sbuf_tensor(f"sxT{i}", [128, 2, T], F32))
                for i in range(4)]
        sq2 = g1.enter_context(nc.sbuf_tensor("sq2", [128, T], BF16))
        srwf = g1.enter_context(nc.sbuf_tensor("srwf", [128, NT, E0], F32))

        def xpl(kk):
            return sxTp[kk // 2][:, kk % 2, :]

        xTr = xT.rearrange("(a p) f -> p a f", p=128)
        for i in range(4):
            q.dma_start(out=sxTp[i][:], in_=xTr[:, 2 * i:2 * i + 2, :])
        q.dma_start(out=srwf[:], in_=rw.rearrange("(a p) f -> p a f", p=128))

        for nn in range(4):
            bank = P0 if nn % 2 == 0 else P1
            for kk in range(NT):
                t.matmul(bank[0:1, 0:512], sonescf[:, :], xpl(kk)[:, nn * 512:(nn + 1) * 512],
                         start=(kk == 0), stop=(kk == NT - 1))
            v.tensor_copy(A0[0:1, nn * 512:(nn + 1) * 512], bank[0:1, 0:512])
        x2banks = [P0[0:1, 0:512], P1[0:1, 0:512], PST[0:1, 0:512], PST[0:1, 512:1024]]
        for kk in range(NT):
            v.tensor_mul(sq2[:, :], xpl(kk)[:, :], xpl(kk)[:, :])
            for nn in range(4):
                t.matmul(x2banks[nn], sonesbc[:, :], sq2[:, nn * 512:(nn + 1) * 512],
                         start=(kk == 0), stop=(kk == NT - 1))
                if kk == NT - 1:
                    v.tensor_copy(A1[0:1, nn * 512:(nn + 1) * 512], x2banks[nn])

        v.tensor_scalar_mul(A0, A0, 1.0 / D)
        v.tensor_scalar_mul(A1, A1, 1.0 / D)
        v.tensor_mul(A3, A0, A0)
        v.tensor_sub(A1, A1, A3)
        v.tensor_scalar_add(A3, A1, EPS)
        s.activation(A2, A3, AF.Sqrt)
        v.reciprocal(A2, A2)
        q.dma_start(out=rows_o[0:1, 6 * T:7 * T], in_=A0)
        q.dma_start(out=rows_o[0:1, 7 * T:8 * T], in_=A2)

        v.tensor_mul(A3, A0, A2)
        v.tensor_scalar_mul(A3, A3, -1.0)
        for hf in range(2):
            sl = slice(hf * 1024, (hf + 1) * 1024)
            r.bcast(PST[:, :], sones[:, :], A2[0:1, sl])
            r.bcast(PE2[:, :], sones[:, :], A3[0:1, sl])
            for a in range(NT):
                v.tensor_mul(stmp0[:, 0:1024], xpl(a)[:, sl], PST[:, 0:1024])
                v.tensor_add(sxnbf[:, a, sl], stmp0[:, 0:1024], PE2[:, 0:1024])
        q.dma_start(out=xnbf_o.rearrange("(a p) f -> p a f", p=128), in_=sxnbf[:])

        for kk in range(NT):
            t.matmul(P0[0:E0, 0:1], srwf[:, kk, :], sonescf[:, :],
                     start=(kk == 0), stop=(kk == NT - 1))
        v.tensor_copy(scsum[:, :], P0[0:E0, 0:1])
        for nn in range(4):
            bank = P0 if nn % 2 == 0 else P1
            for kk in range(NT):
                t.matmul(bank[0:E0, 0:512], srwf[:, kk, :], xpl(kk)[:, nn * 512:(nn + 1) * 512],
                         start=(kk == 0), stop=(kk == NT - 1))
            v.tensor_copy(slog[:, nn * 512:(nn + 1) * 512], bank[0:E0, 0:512])
        for hf in range(2):
            sl = slog[:, hf * 1024:(hf + 1) * 1024]
            r.bcast(PST[0:E0, :], sones[0:1, 0:E0], A2[0:1, hf * 1024:(hf + 1) * 1024])
            v.tensor_mul(sl, sl, PST[0:E0, 0:1024])
            r.bcast(PE2[0:E0, :], sones[0:1, 0:E0], A3[0:1, hf * 1024:(hf + 1) * 1024])
            v.scalar_tensor_tensor(sl, PE2[0:E0, 0:1024], scsum[:, :], sl, ALU.mult, ALU.add)
            v.tensor_scalar_add(sl, sl, srb[:, :])

        v.tensor_scalar_add(A3, A1, EPS)
        v.reciprocal(A3, A3)
        v.tensor_mul(A1, A1, A3)
        v.tensor_scalar_add(A1, A1, EPS)
        s.activation(A1, A1, AF.Sqrt)
        v.reciprocal(A1, A1)

        for c in range(NC):
            bank = P0 if c % 2 == 0 else P1
            t.transpose(bank[0:128, 0:E0], slog[:, c * 128:(c + 1) * 128], sident16[:, :])
            v.tensor_copy(slogtm[:, c, :], bank[0:128, 0:E0])
        g1.close()
        r.fence()

        def p_top2a(c):
            def run():
                ch = slogtm[:, c, :]
                w16 = sw16[:, c, :]
                wk = stk
                v.max(out=w16[:, 0:8], in_=ch)
                v.tensor_scalar_mul(w16[:, 8:9], w16[:, 0:1], -1.0)
                v.tensor_single_scalar(wk[:, 0:16], ch, w16[:, 0:1], ALU.is_ge)
                v.tensor_tensor_scan(wk[:, 16:32], wk[:, 0:16], szer16[:, 0:E0], 0.0, ALU.add, ALU.add)
                v.tensor_single_scalar(wk[:, 16:32], wk[:, 16:32], 1.0, ALU.is_equal)
                v.tensor_mul(wk[:, 0:16], wk[:, 0:16], wk[:, 16:32])
                v.tensor_mul(wk[:, 32:48], wk[:, 0:16], siota16r[:, :])
                v.reduce_sum(spk[:, c, 0:1], wk[:, 32:48], axis=AX.X)
                v.tensor_single_scalar(wk[:, 16:32], ch, w16[:, 1:2], ALU.is_ge)
                v.tensor_tensor_scan(wk[:, 32:48], wk[:, 16:32], szer16[:, 0:E0], 0.0, ALU.add, ALU.add)
                v.tensor_single_scalar(wk[:, 32:48], wk[:, 32:48], 2.0, ALU.is_le)
                v.tensor_mul(wk[:, 16:32], wk[:, 16:32], wk[:, 32:48])
                v.tensor_sub(wk[:, 16:32], wk[:, 16:32], wk[:, 0:16])
                v.tensor_mul(wk[:, 16:32], wk[:, 16:32], siota16r[:, :])
                v.reduce_sum(spk[:, c, 1:2], wk[:, 16:32], axis=AX.X)
            return run

        def p_exp(c):
            def run():
                ch = slogtm[:, c, :]
                w16 = sw16[:, c, :]
                s.activation(stk[:, 0:E0], ch, AF.Exp, bias=w16[:, 8:9], accum_out=w16[:, 9:10])
                s.activation(w16[:, 10:11], w16[:, 1:2], AF.Exp, bias=w16[:, 8:9])
                v.tensor_scalar_mul(w16[:, 11:12], w16[:, 9:10], 1e-9)
                v.tensor_add(w16[:, 11:12], w16[:, 11:12], w16[:, 10:11])
                v.tensor_scalar_add(w16[:, 11:12], w16[:, 11:12], 1.0)
                v.reciprocal(spk[:, c, 2:3], w16[:, 11:12])
                v.tensor_mul(spk[:, c, 3:4], w16[:, 10:11], spk[:, c, 2:3])
            return run

        def p_pack():
            for c in range(NC):
                bank = P0 if c % 2 == 0 else P1
                t.transpose(bank[0:4, 0:128], spk[:, c, :], sid128[:, :])
                v.tensor_copy(str4[0:4, c * 128:(c + 1) * 128], bank[0:4, 0:128])
            q.dma_start(out=rows_o[0:1, 0:4 * T].rearrange("a (x f) -> a x f", x=4),
                        in_=str4[0:4, :].unsqueeze(1))
            q.dma_start(out=ohC[0:1, 0:T], in_=str4[1:2, :])

        def p_onehot():
            for (dst, src) in ((ohA, str4[0:1, :]), (ohB, ohC[0:1, 0:T])):
                for hf in range(2):
                    for (bk, o) in ((P0, 0), (P1, 512)):
                        t.matmul(bk[0:E0, 0:512], sones[0:1, 0:E0],
                                 src[0:1, hf * 1024 + o:hf * 1024 + o + 512],
                                 start=True, stop=True)
                        v.tensor_scalar(dst[:, hf * 1024 + o:hf * 1024 + o + 512],
                                        bk[0:E0, 0:512], siota16c[:, :], None, op0=ALU.is_equal)

        def p_scan():
            v.tensor_add(slog[:, :], ohA[:, :], ohB[:, :])
            v.memset(ohC[:, :], 0.0)
            v.tensor_tensor_scan(ohC[:, :], slog[:, :], ohC[:, :], 0.0, ALU.add, ALU.add)
            v.tensor_copy(scnt[:, :], ohC[:, T - 1:T])
            q.dma_start(out=cnt_o[:, :], in_=scnt[:, :])
            v.tensor_sub(slog[:, :], ohC[:, :], slog[:, :])
            v.tensor_sub(ohC[:, :], ohC[:, :], ohB[:, :])
            v.tensor_mul(slog[:, :], slog[:, :], ohA[:, :])
            v.tensor_mul(ohC[:, :], ohC[:, :], ohB[:, :])

        def p_extract(roff, src):
            def run():
                for nn in range(4):
                    bank = P0 if nn % 2 == 0 else P1
                    t.matmul(bank[0:1, 0:512], sonescf[0:E0, :], src[:, nn * 512:(nn + 1) * 512],
                             start=True, stop=True)
                    v.tensor_copy(A3[0:1, nn * 512:(nn + 1) * 512], bank[0:1, 0:512])
                q.dma_start(out=rows_o[0:1, roff * T:(roff + 1) * T], in_=A3)
                v.memset(A3[0:1, 0:1], 0.0)
            return run

        pieces_v = [p_top2a(c) for c in range(NC)]
        pieces_s = [(c + 1, p_exp(c)) for c in range(NC)]
        pieces_s += [(NC, p_pack), (NC, p_onehot), (NC, p_scan),
                     (NC, p_extract(4, slog)), (NC, p_extract(5, ohC))]

        sswg = e1(nc.sbuf_tensor("sswg", [128, NT, F0], BF16))
        sswu = e1(nc.sbuf_tensor("sswu", [128, NT, F0], BF16))
        sswd = e1(nc.sbuf_tensor("sswd", [128, NF, D], BF16))
        sh1 = e1(nc.sbuf_tensor("sh1", [128, NF, 512], BF16))
        sstg = e1(nc.sbuf_tensor("sstg", [128, 2, 512], BF16))
        g.dma_start(out=sswg[:], in_=swg.rearrange("(a p) f -> p a f", p=128))
        g.dma_start(out=sswu[:], in_=swu.rearrange("(a p) f -> p a f", p=128))
        g.dma_start(out=sswd[:], in_=swd.rearrange("(a p) f -> p a f", p=128))
        BKS = [PST[:, 0:512], PST[:, 512:1024], PE2[:, 0:512], PE2[:, 512:1024],
               PF[:, 0:512], PF[:, 512:1024]]
        ngrp = 0

        def piece_tick(in_silu):
            nonlocal ngrp
            ngrp += 1
            if ngrp % 2 != 0:
                return
            done_a = NC - len(pieces_v)
            if not in_silu and pieces_s and pieces_s[0][0] <= done_a:
                pieces_s.pop(0)[1]()
            elif pieces_v:
                pieces_v.pop(0)()

        for nn in range(4):
            tsl = slice(nn * 512, (nn + 1) * 512)
            for m in range(NF):
                bank = BKS[m % 6]
                for kk in range(NT):
                    t.matmul(bank[:, :], sswg[:, kk, m * 128:(m + 1) * 128], sxnbf[:, kk, tsl],
                             start=(kk == 0), stop=(kk == NT - 1))
                s.activation(sh1[:, m, :], bank[:, :], AF.Silu)
                piece_tick(True)
            for m in range(NF):
                bank = BKS[m % 6]
                for kk in range(NT):
                    t.matmul(bank[:, :], sswu[:, kk, m * 128:(m + 1) * 128], sxnbf[:, kk, tsl],
                             start=(kk == 0), stop=(kk == NT - 1))
                v.tensor_mul(sh1[:, m, :], sh1[:, m, :], bank[:, :])
                piece_tick(False)
            for m in range(NT):
                bank = BKS[m % 6]
                stg = sstg[:, m % 2, :]
                for kk in range(NF):
                    t.matmul(bank[:, :], sswd[:, kk, m * 128:(m + 1) * 128], sh1[:, kk, :],
                             start=(kk == 0), stop=(kk == NF - 1))
                v.tensor_copy(stg, bank[:, :])
                q.dma_start(out=shT_o[m * 128:(m + 1) * 128, tsl], in_=stg)
                piece_tick(False)
        for pc in pieces_v:
            pc()
        for _, pc in pieces_s:
            pc()

    r.fence()
    sxq_guard = nc.sbuf_tensor("sxq", [128, NT, T], BF16, side="right")
    sxq = sxq_guard.__enter__()
    for hf in range(2):
        sl = slice(hf * 1024, (hf + 1) * 1024)
        r.bcast(PST[:, :], sones[:, :], A1[0:1, sl])
        for a in range(NT):
            v.tensor_mul(stmp0[:, 0:1024], sxnbf[:, a, sl], PST[:, 0:1024])
            v.tensor_scalar(sxq[:, a, sl], stmp0[:, 0:1024], sbias[:, a, 0:1],
                            sbias[:, a, 1:2], op0=ALU.mult, op1=ALU.add)
    sxnbf_scope.close()
    snegM = e(nc.sbuf_tensor("snegM", [128, NH], F32))

    r.fence()
    svtok = lctx1 = nc.sbuf_tensor("svtok", [128, NC, NH * (HD + 1)], BF16, side="left")
    svtok = lctx1.__enter__()
    lctx2 = nc.sbuf_tensor("skpk", [128, NT, T], BF16, side="left")
    skpk = lctx2.__enter__()
    lctx3 = nc.sbuf_tensor("sqpk", [128, NT, T], BF16, side="left")
    sqpk = lctx3.__enter__()
    with ExitStack() as ph:
        e4 = ph.enter_context
        sWa = e4(nc.sbuf_tensor("sWa", [128, NT, D], BF16))
        sspr = e4(nc.sbuf_tensor("sspr", [16, 128], F32))
        sbqR = e4(nc.sbuf_tensor("sbqR", [HD, NH], F32))
        sbkR = e4(nc.sbuf_tensor("sbkR", [HD, NH], F32))
        sbvB = e4(nc.sbuf_tensor("sbvB", [128, D], F32))
        shsc = e4(nc.sbuf_tensor("shsc", [HD, T], BF16))
        sqsq = e4(nc.sbuf_tensor("sqsq", [HD, T], BF16))
        q.dma_start(out=sbqR[:], in_=bqR[:, :])
        q.dma_start(out=sbkR[:], in_=bkR[:, :])
        q.dma_start(out=sbvB[:], in_=bvB[:, :])
        for (W, bR, dstpk) in ((Wq, sbqR, sqpk), (Wk, sbkR, skpk)):
            g.dma_start(out=sWa[:], in_=W.rearrange("(a p) f -> p a f", p=128))
            for h in range(NH):
                for nn in range(4):
                    bank = BK6[(h * 4 + nn) % 6]
                    for kk in range(NT):
                        t.matmul(bank[0:HD, 0:512], sWa[:, kk, h * HD:(h + 1) * HD],
                                 sxq[:, kk, nn * 512:(nn + 1) * 512],
                                 start=(kk == 0), stop=(kk == NT - 1))
                    v.tensor_scalar(shsc[:, nn * 512:(nn + 1) * 512], bank[0:HD, 0:512],
                                    bR[:, h:h + 1], None, op0=ALU.add)
                s.activation(sqsq[:, :], shsc[:, :], AF.Square)
                for nn in range(4):
                    bank = P0 if nn % 2 == 0 else P1
                    t.matmul(bank[0:1, 0:512], sonesbc[0:HD, :],
                             sqsq[:, nn * 512:(nn + 1) * 512], start=True, stop=True)
                    v.tensor_copy(A3[0:1, nn * 512:(nn + 1) * 512], bank[0:1, 0:512])
                g.dma_start(out=sspr[:, :], in_=A3)
                s.activation(sspr[:, :], sspr[:, :], AF.Sqrt)
                v.tensor_scalar_max(sspr[:, :], sspr[:, :], 1e-12)
                v.reciprocal(sspr[:, :], sspr[:, :])
                g.dma_start(out=A3, in_=sspr[:, :])
                for hf in range(2):
                    sl = slice(hf * 1024, (hf + 1) * 1024)
                    r.bcast(PST[0:HD, :], sones[0:1, 0:HD], A3[0:1, sl])
                    v.tensor_mul(shsc[:, sl], shsc[:, sl], PST[0:HD, 0:1024])
                g.dma_start(out=dstpk[(h % 2) * HD:(h % 2 + 1) * HD, h // 2, :], in_=shsc[:, :])
        g.dma_start(out=sWa[:], in_=Wv.rearrange("(a p) f -> p a f", p=128))
        v.memset(svtok[:].rearrange("p c (h j) -> p c h j", j=HD + 1)[:, :, :, HD:HD + 1], 1.0)
        for c in range(NC):
            for nn in range(2):
                bank = BK6[(c * 2 + nn) % 6]
                for kk in range(NT):
                    t.matmul(bank[:, :], sxq[:, kk, c * 128:(c + 1) * 128],
                             sWa[:, kk, nn * 512:(nn + 1) * 512],
                             start=(kk == 0), stop=(kk == NT - 1))
                v.tensor_add(svtok[:, c, :].rearrange("p (h j) -> p h j", j=HD + 1)[:, nn * 8:(nn + 1) * 8, 0:HD],
                             bank[:, :].rearrange("p (h j) -> p h j", j=HD),
                             sbvB[:, nn * 512:(nn + 1) * 512].rearrange("p (h j) -> p h j", j=HD))
    sxq_guard.__exit__(None, None, None)

    r.fence()
    def attn_step(step, srcpk, dstpk, negM=None, mknegM=None):
        with ExitStack() as ph:
            e5 = ph.enter_context
            sE = [e5(nc.sbuf_tensor(f"sE{step}_{j}", [128, T], BF16)) for j in range(2)]
            sAev = [e5(nc.sbuf_tensor(f"sAev{step}_{j}", [HD, T], BF16)) for j in range(2)]
            sspr5 = [e5(nc.sbuf_tensor(f"sspr{step}_{j}", [8, 128], F32)) for j in range(2)]
            srowb = [e5(nc.sbuf_tensor(f"srowb{step}_{j}", [2, T], BF16)) for j in range(2)]
            shi = [e5(nc.sbuf_tensor(f"shi{step}_{j}", [8, 128], BF16)) for j in range(2)]
            slo = [e5(nc.sbuf_tensor(f"slo{step}_{j}", [8, 128], BF16)) for j in range(2)]
            sden = [e5(nc.sbuf_tensor(f"sden{step}_{j}", [65, 1024], F32)) for j in range(2)]
            sones2b = e5(nc.sbuf_tensor(f"sones2b{step}", [2, HD], BF16))
            v.memset(sones2b[:], 1.0)
            PSIM = [PE2, PF]

            def make_tail(p, pp):
                def tail():
                    for j in range(2):
                        h = 2 * p + j
                        for hf in range(2):
                            qs = slice(hf * 1024, (hf + 1) * 1024)
                            for o in (0, 512):
                                t.matmul(PSIM[j][0:HD, o:o + 512], sones2b[:, :],
                                         srowb[j][:, hf * 1024 + o:hf * 1024 + o + 512],
                                         start=True, stop=True)
                            v.tensor_mul(sAev[j][:, qs], sAev[j][:, qs], PSIM[j][0:HD, 0:1024])
                        if mknegM is not None:
                            v.tensor_mul(sE[j][0:HD, :], sAev[j][:, :], sAev[j][:, :])
                            regs = [PSIM[1 - j][0:1, 0:512], PSIM[1 - j][0:1, 512:1024],
                                    PSIM[j][0:1, 0:512], PSIM[j][0:1, 512:1024]]
                            for nn in range(4):
                                t.matmul(regs[nn], sonesbc[0:HD, :],
                                         sE[j][0:HD, nn * 512:(nn + 1) * 512],
                                         start=True, stop=True)
                            for nn in range(4):
                                v.reduce_max(A3[0:1, 80 + 4 * j + nn:81 + 4 * j + nn],
                                             regs[nn], axis=AX.X)
                            v.reduce_max(A3[0:1, 64 + h:65 + h],
                                         A3[0:1, 80 + 4 * j:84 + 4 * j], axis=AX.X)
                        (q if j == 0 else g).dma_start(
                            out=dstpk[(h % 2) * HD:(h % 2 + 1) * HD, h // 2, :],
                            in_=sAev[j][:, :])
                return tail

            tail = None
            for p in range(NH // 2):
                pp = p % 2
                for hf in range(2):
                    qs = slice(hf * 1024, (hf + 1) * 1024)

                    def sim_mm(j, kc):
                        base = j * HD
                        for nn in range(2):
                            t.matmul(PSIM[j][:, nn * 512:(nn + 1) * 512],
                                     skpk[base:base + HD, p, kc * 128:(kc + 1) * 128],
                                     srcpk[base:base + HD, p,
                                           hf * 1024 + nn * 512:hf * 1024 + (nn + 1) * 512],
                                     start=True, stop=True, tile_position=(base, 0))
                    sim_mm(0, 0)
                    sim_mm(1, 0)
                    for kc in range(NC):
                        for j in range(2):
                            h = 2 * p + j
                            eb = sE[j][:, 0:1024] if kc % 2 == 0 else sE[j][:, 1024:2048]
                            if negM is None:
                                s.activation(eb, PSIM[j][:, 0:1024], AF.Exp)
                            else:
                                s.activation(eb, PSIM[j][:, 0:1024], AF.Exp, bias=negM[:, h:h + 1])
                            if kc + 1 < NC:
                                sim_mm(j, kc + 1)
                            for nn in range(2):
                                pav = (PST[0:65, nn * 512:(nn + 1) * 512] if j == 0
                                       else (P0 if nn == 0 else P1)[0:65, 0:512])
                                t.matmul(pav, svtok[:, kc, h * (HD + 1):(h + 1) * (HD + 1)],
                                         eb[:, nn * 512:(nn + 1) * 512],
                                         start=(kc == 0), stop=(kc == NC - 1))
                            if kc + 1 < NC:
                                base = j * HD
                                t.matmul(PSIM[j][:, 0:512],
                                         skpk[base:base + HD, p, (kc + 1) * 128:(kc + 2) * 128],
                                         srcpk[base:base + HD, p, hf * 1024:hf * 1024 + 512],
                                         start=True, stop=True, tile_position=(base, 0))
                    if hf == 0 and tail is not None:
                        tail()
                        tail = None
                    v.tensor_copy(sAev[0][:, qs], PST[0:HD, 0:1024])
                    s.activation(sAev[1][:, hf * 1024:hf * 1024 + 512], P0[0:HD, 0:512], AF.Copy, scale=1.0)
                    s.activation(sAev[1][:, hf * 1024 + 512:hf * 1024 + 1024], P1[0:HD, 0:512], AF.Copy, scale=1.0)
                    s.activation(sden[0][64:65, 0:1024], PST[64:65, 0:1024], AF.Copy, scale=1.0)
                    v.tensor_copy(sden[1][64:65, 0:512], P0[64:65, 0:512])
                    v.tensor_copy(sden[1][64:65, 512:1024], P1[64:65, 0:512])
                    g.dma_start(out=sspr5[0][:, :], in_=sden[0][64:65, 0:1024])
                    q.dma_start(out=sspr5[1][:, :], in_=sden[1][64:65, 0:1024])
                    for j, eng in ((0, g), (1, q)):
                        v.reciprocal(sspr5[j][:, :], sspr5[j][:, :])
                        v.tensor_copy(shi[j][:, :], sspr5[j][:, :])
                        v.tensor_sub(sspr5[j][:, :], sspr5[j][:, :], shi[j][:, :])
                        v.tensor_copy(slo[j][:, :], sspr5[j][:, :])
                        eng.dma_start(out=srowb[j][0:1, qs], in_=shi[j][:, :])
                        eng.dma_start(out=srowb[j][1:2, qs], in_=slo[j][:, :])
                tail = make_tail(p, pp)
            tail()
            if mknegM is not None:
                s.activation(A3[0:1, 64:64 + NH], A3[0:1, 64:64 + NH], AF.Sqrt)
                v.tensor_scalar_mul(A3[0:1, 64:64 + NH], A3[0:1, 64:64 + NH], -1.0)
                t.matmul(PSIM[0][0:128, 0:NH], sones[0:1, :], A3[0:1, 64:64 + NH],
                         start=True, stop=True)
                v.tensor_copy(mknegM[:, :], PSIM[0][0:128, 0:NH])

    attn_step(0, sqpk, sqpk, negM=None, mknegM=snegM)
    r.fence()
    attn_step(1, sqpk, sqpk, negM=snegM, mknegM=None)

    r.fence()
    with ExitStack() as ph:
        e6 = ph.enter_context
        sWoR = e6(nc.sbuf_tensor("sWoR", [HD, NH * D], BF16))
        o_acc = e6(nc.sbuf_tensor("o_acc", [128, NT, T], BF16))
        scr0 = e6(nc.sbuf_tensor("scr0", [HD, T], BF16))
        scr1 = e6(nc.sbuf_tensor("scr1", [HD, T], BF16))
        g.dma_start(out=sWoR[:], in_=WoR[:, :])
        for grp in range(4):
            heads = [4 * grp + i for i in range(4)]
            up0, up1 = heads[1], heads[3]
            g.dma_start(out=scr0[:, :], in_=sqpk[HD:128, up0 // 2, :])
            g.dma_start(out=scr1[:, :], in_=sqpk[HD:128, up1 // 2, :])
            srcs = {heads[0]: sqpk[0:HD, heads[0] // 2, :],
                    heads[1]: scr0[:, :],
                    heads[2]: sqpk[0:HD, heads[2] // 2, :],
                    heads[3]: scr1[:, :]}
            for m in range(NT):
                for nn in range(4):
                    bank = BK6[(m * 4 + nn) % 6]
                    for ih, h in enumerate(heads):
                        t.matmul(bank[:, :], sWoR[:, h * D + m * 128:h * D + (m + 1) * 128],
                                 srcs[h][:, nn * 512:(nn + 1) * 512],
                                 start=(ih == 0), stop=(ih == 3))
                    if grp == 0:
                        v.tensor_scalar(o_acc[:, m, nn * 512:(nn + 1) * 512], bank[:, :],
                                        sbias[:, m, 2:3], None, op0=ALU.add)
                    else:
                        v.tensor_add(o_acc[:, m, nn * 512:(nn + 1) * 512],
                                     o_acc[:, m, nn * 512:(nn + 1) * 512], bank[:, :])
        q.dma_start(out=oT_o.rearrange("(a p) f -> p a f", p=128), in_=o_acc[:])

    lctx3.__exit__(None, None, None)
    lctx2.__exit__(None, None, None)
    lctx1.__exit__(None, None, None)
    ctx.close()
    ectx = ExitStack()
    with ectx:
        r.emit(ectx)
    return nc





def build_B():
    nc = bass.Bass("TRN2", target_bir_lowering=False, detect_race_conditions=False)
    dtr = nc.dram_tensor
    I = lambda n, s, d=F32: dtr(n, s, d, kind="ExternalInput").ap()
    O = lambda n, s, d=F32: dtr(n, s, d, kind="ExternalOutput").ap()

    x_tok = I("x_tok", [T, D])
    o_tok = I("o_tok", [T, D], BF16)
    sh_tok = I("sh_tok", [T, D], BF16)
    xnbf_tok = I("xnbf_tok", [T, D], BF16)
    rows_in = I("rows_in", [1, 8 * T])
    cnts = I("cnts", [8, E0])
    coremask = I("coremask", [8, 1])
    wg_h = I("wg_h", [E0, D, F0]); wu_h = I("wu_h", [E0, D, F0]); wd_h = I("wd_h", [E0, F0, D])
    wq1 = I("wq1", [D, 128]); wk1 = I("wk1", [D, 128]); wv1 = I("wv1", [D, 128])
    wo1 = I("wo1", [128, D])
    Wg1 = I("Wg1", [D, E1])
    We1 = I("We1", [D, D])
    be_b = I("be_b", [128, D])
    hdm = I("hdm", [128, 1], I32)
    onesr = I("onesr", [1, 128])
    onescf = I("onescf", [128, 1])
    onesbc = I("onesbc", [128, 1], BF16)
    iota8c = I("iota8c", [E1, 1])
    iota8r = I("iota8r", [128, E1])
    ident8 = I("ident8", [E1, E1])
    id128 = I("id128", [128, 128])
    id128b = I("id128b", [128, 128], BF16)
    iota16c = I("iota16c", [E0, 1])

    x1_o = O("x1", [T, D])
    z_o = O("z_tok", [T, D], BF16)
    s1_o = O("s1_tok", [T, D], BF16)
    rows1_o = O("rows1", [1, 2 * T])
    cnt1_o = O("cnt1", [E1, 1])

    bkt = dtr("bkt", [NSLOT + 1, D], BF16, kind="Internal").ap()
    BO = dtr("BO", [NSLOT + 1, D], BF16, kind="Internal").ap()
    xh_h = dtr("xh_h", [T, D], BF16, kind="Internal").ap()
    xl_h = dtr("xl_h", [T, D], BF16, kind="Internal").ap()

    r = Rec(nc)
    q, g, v, s, t = r.q, r.g, r.v, r.s, r.t
    ctx = ExitStack()
    e = ctx.enter_context

    ar = e(nc.sbuf_tensor("arena", [1, 4 * T], F32))
    A0 = ar[0:1, 0:T]; A1 = ar[0:1, T:2 * T]; A2 = ar[0:1, 2 * T:3 * T]; A3 = ar[0:1, 3 * T:4 * T]
    sones = e(nc.sbuf_tensor("sones", [1, 128], F32))
    sonescf = e(nc.sbuf_tensor("sonescf", [128, 1], F32))
    sonesbc = e(nc.sbuf_tensor("sonesbc", [128, 1], BF16))
    siota8c = e(nc.sbuf_tensor("siota8c", [E1, 1], F32))
    siota8r = e(nc.sbuf_tensor("siota8r", [128, E1], F32))
    sident8 = e(nc.sbuf_tensor("sident8", [E1, E1], F32))
    sid128 = e(nc.sbuf_tensor("sid128", [128, 128], F32))
    sid128b = e(nc.sbuf_tensor("sid128b", [128, 128], BF16))
    siota16c = e(nc.sbuf_tensor("siota16c", [E0, 1], F32))
    scol = e(nc.sbuf_tensor("scol", [128, NC, 8], F32))
    sicol = e(nc.sbuf_tensor("sicol", [128, NC, 2], I32))
    scnt1 = e(nc.sbuf_tensor("scnt1", [E1, 1], F32))

    P0 = e(nc.psum_tensor("P0", [128, 512], F32))
    P1 = e(nc.psum_tensor("P1", [128, 512], F32))
    PST = e(nc.psum_tensor("PST", [128, 1024], F32))
    PE2 = e(nc.psum_tensor("PE2", [128, 1024], F32))
    PF = e(nc.psum_tensor("PF", [128, 1024], F32))
    BK6 = [P0[:, 0:512], P1[:, 0:512], PE2[:, 0:512], PE2[:, 512:1024],
           PF[:, 0:512], PF[:, 512:1024]]

    q.dma_start(out=sones[:], in_=onesr[:, :])
    q.dma_start(out=sonescf[:], in_=onescf[:, :])
    q.dma_start(out=sonesbc[:], in_=onesbc[:, :])
    q.dma_start(out=siota8c[:], in_=iota8c[:, :])
    q.dma_start(out=siota8r[:], in_=iota8r[:, :])
    q.dma_start(out=sident8[:], in_=ident8[:, :])
    q.dma_start(out=sid128[:], in_=id128[:, :])
    q.dma_start(out=sid128b[:], in_=id128b[:, :])
    q.dma_start(out=siota16c[:], in_=iota16c[:, :])

    def row_to_col(row, colap):
        for c in range(NC):
            bank = P0 if c % 2 == 0 else P1
            t.transpose(bank[0:128, 0:1], row[0:1, c * 128:(c + 1) * 128], sid128[0:1, 0:1])
            v.tensor_copy(colap[:, c, :], bank[0:128, 0:1])

    def col_to_row(colap, row):
        for c in range(NC):
            bank = P0 if c % 2 == 0 else P1
            t.transpose(bank[0:1, 0:128], colap[:, c, :], sid128[:, :])
            v.tensor_copy(row[0:1, c * 128:(c + 1) * 128], bank[0:1, 0:128])

    gW = ExitStack()
    swg = [gW.enter_context(nc.sbuf_tensor(f"swg{i}", [128, NT, F0], BF16)) for i in range(2)]
    swu = [gW.enter_context(nc.sbuf_tensor(f"swu{i}", [128, NT, F0], BF16)) for i in range(2)]
    swd = [gW.enter_context(nc.sbuf_tensor(f"swd{i}", [128, NF, D], BF16)) for i in range(2)]
    for i in range(2):
        g.dma_start(out=swg[i][:], in_=wg_h[i].rearrange("(a p) f -> p a f", p=128))
        g.dma_start(out=swu[i][:], in_=wu_h[i].rearrange("(a p) f -> p a f", p=128))
        g.dma_start(out=swd[i][:], in_=wd_h[i].rearrange("(a p) f -> p a f", p=128))

    with ExitStack() as ph:
        e0 = ph.enter_context
        soffs = e0(nc.sbuf_tensor("soffs", [E0, 1], F32))
        scnts = e0(nc.sbuf_tensor("scnts", [8, E0], F32))
        scm = e0(nc.sbuf_tensor("scm", [8, 1], F32))
        ohA = e0(nc.sbuf_tensor("ohA", [E0, T], F32))
        ohB = e0(nc.sbuf_tensor("ohB", [E0, T], F32))
        ohS = e0(nc.sbuf_tensor("ohS", [E0, T], F32))
        ohC = e0(nc.sbuf_tensor("ohC", [E0, T], F32))
        sxchd = [e0(nc.sbuf_tensor(f"sxch{i}", [128, 1024], BF16)) for i in range(2)]
        v.memset(sxchd[0][:, :], 0.0)
        for i0 in range(0, NSLOT + 1, 128):
            n = min(128, NSLOT + 1 - i0)
            s.dma_start(out=bkt[i0:i0 + n, :], in_=sxchd[0][0:n, :])
        s.dma_start(out=BO[NSLOT:NSLOT + 1, :], in_=sxchd[0][0:1, :])
        q.dma_start(out=scnts[:], in_=cnts[:, :])
        q.dma_start(out=scm[:], in_=coremask[:, :])
        t.matmul(P0[0:E0, 0:1], scnts[:, :], scm[:, :], start=True, stop=True)
        v.tensor_copy(soffs[:, :], P0[0:E0, 0:1])
        q.dma_start(out=A0, in_=rows_in[0:1, 0:T])
        q.dma_start(out=A1, in_=rows_in[0:1, T:2 * T])
        for (dst, src) in ((ohA, A0), (ohB, A1)):
            for hf in range(2):
                r.bcast(PST[0:E0, :], sones[0:1, 0:E0], src[0:1, hf * 1024:(hf + 1) * 1024])
                v.tensor_scalar(dst[:, hf * 1024:(hf + 1) * 1024], PST[0:E0, 0:1024],
                                siota16c[:, :], None, op0=ALU.is_equal)
        for (roff, oh, msk, wplane) in ((4, ohA, ohS, 2), (5, ohB, ohC, 3)):
            q.dma_start(out=A2, in_=rows_in[0:1, roff * T:(roff + 1) * T])
            for nn in range(4):
                bank = P0 if nn % 2 == 0 else P1
                t.matmul(bank[0:1, 0:512], soffs[:, :], oh[:, nn * 512:(nn + 1) * 512],
                         start=True, stop=True)
                v.tensor_copy(A3[0:1, nn * 512:(nn + 1) * 512], bank[0:1, 0:512])
            v.tensor_add(A2, A2, A3)
            v.tensor_single_scalar(A3, A2, float(C0), ALU.is_lt)
            q.dma_start(out=A2, in_=rows_in[0:1, (roff - 2) * T:(roff - 1) * T])
            v.tensor_mul(A2, A2, A3)
            row_to_col(A2, scol[:, :, wplane:wplane + 1])
            for hf in range(2):
                r.bcast(PST[0:E0, :], sones[0:1, 0:E0], A3[0:1, hf * 1024:(hf + 1) * 1024])
                v.tensor_mul(msk[:, hf * 1024:(hf + 1) * 1024],
                             oh[:, hf * 1024:(hf + 1) * 1024], PST[0:E0, 0:1024])
        v.tensor_add(ohA[:, :], ohS[:, :], ohC[:, :])
        v.memset(ohB[:, :], 0.0)
        v.tensor_tensor_scan(ohB[:, :], ohA[:, :], ohB[:, :], 0.0, ALU.add, ALU.add)
        v.tensor_sub(ohA[:, :], ohB[:, :], ohA[:, :])
        v.tensor_sub(ohB[:, :], ohB[:, :], ohC[:, :])
        v.tensor_mul(ohA[:, :], ohA[:, :], ohS[:, :])
        v.tensor_mul(ohB[:, :], ohB[:, :], ohC[:, :])
        for (kk_, ohm, sel, plane) in ((0, ohS, ohA, 0), (1, ohC, ohB, 1)):
            for nn in range(4):
                bank = P0 if nn % 2 == 0 else P1
                t.matmul(bank[0:1, 0:512], sonescf[0:E0, :], sel[:, nn * 512:(nn + 1) * 512],
                         start=True, stop=True)
                v.tensor_copy(A2[0:1, nn * 512:(nn + 1) * 512], bank[0:1, 0:512])
            for nn in range(4):
                bank = P0 if nn % 2 == 0 else P1
                t.matmul(bank[0:1, 0:512], sonescf[0:E0, :], ohm[:, nn * 512:(nn + 1) * 512],
                         start=True, stop=True)
                v.tensor_copy(A3[0:1, nn * 512:(nn + 1) * 512], bank[0:1, 0:512])
            v.tensor_single_scalar(A0, A2, float(CL), ALU.is_lt)
            v.tensor_mul(A3, A3, A0)
            q.dma_start(out=A0, in_=rows_in[0:1, kk_ * T:(kk_ + 1) * T])
            v.tensor_scalar_mul(A0, A0, float(CL))
            v.tensor_add(A0, A0, A2)
            v.tensor_scalar_sub(A0, A0, float(NSLOT))
            v.tensor_mul(A0, A0, A3)
            v.tensor_scalar_add(A0, A0, float(NSLOT))
            row_to_col(A0, scol[:, :, plane:plane + 1])
        v.tensor_copy(sicol[:, :, :], scol[:, :, 0:2])
        q.dma_start(out=A0, in_=rows_in[0:1, 6 * T:7 * T])
        q.dma_start(out=A1, in_=rows_in[0:1, 7 * T:8 * T])
        v.tensor_mul(A2, A0, A1)
        v.tensor_scalar_mul(A2, A2, -1.0)
        row_to_col(A2, scol[:, :, 4:5])
        row_to_col(A1, scol[:, :, 5:6])
        for c in range(NC):
            xb = sxchd[c % 2]
            q.dma_start(out=xb[:, :], in_=xnbf_tok[c * 128:(c + 1) * 128, :])
            g.indirect_dma_start(bkt[:, :], IndirectOffsetOnAxis(sicol[:, c, 0:1], 0), xb[:, :], None)
            g.indirect_dma_start(bkt[:, :], IndirectOffsetOnAxis(sicol[:, c, 1:2], 0), xb[:, :], None)

    r.fence()
    with ExitStack() as ph:
        e1 = ph.enter_context
        sbk = e1(nc.sbuf_tensor("sbk", [128, 2, 1024], BF16))
        sbkT = [e1(nc.sbuf_tensor(f"sbkT{i}", [128, NT, CL], BF16)) for i in range(2)]
        sh1 = [e1(nc.sbuf_tensor(f"sh1{i}", [128, NF, CL], BF16)) for i in range(2)]
        sbo = e1(nc.sbuf_tensor("sbo", [128, 1024], BF16))
        BK4 = [P0[:, 0:512], P1[:, 0:512], PST[:, 0:512], PST[:, 512:1024]]

        def bucket_load(ei, buf):
            q.dma_start(out=sbk[:, 0, :], in_=bkt[ei * CL:ei * CL + 128, :])
            q.dma_start(out=sbk[:, 1, :], in_=bkt[ei * CL + 128:ei * CL + 256, :])

        def transp_steps(ei, buf):
            for cc in range(4):
                n = 128 if cc < 3 else CL - 384
                bank = (PE2 if cc % 2 == 0 else PF)[:, :].bitcast(BF16)
                for j in range(NT):
                    t.transpose(bank[0:128, j * 128:j * 128 + n],
                                sbk[0:n, cc % 2, j * 128:(j + 1) * 128], sid128b[0:n, 0:n])
                for j in range(NT):
                    v.tensor_copy(sbkT[buf][:, j, cc * 128:cc * 128 + n],
                                  bank[0:128, j * 128:j * 128 + n])
                if cc == 1:
                    q.dma_start(out=sbk[:, 0, :], in_=bkt[ei * CL + 256:ei * CL + 384, :])
                    q.dma_start(out=sbk[0:CL - 384, 1, :], in_=bkt[ei * CL + 384:ei * CL + CL, :])
                yield

        def wload(ei, buf):
            g.dma_start(out=swg[buf][:], in_=wg_h[ei].rearrange("(a p) f -> p a f", p=128))
            g.dma_start(out=swu[buf][:], in_=wu_h[ei].rearrange("(a p) f -> p a f", p=128))
            g.dma_start(out=swd[buf][:], in_=wd_h[ei].rearrange("(a p) f -> p a f", p=128))

        rot = [0]

        def nbank():
            b = BK4[rot[0] % 4]
            rot[0] += 1
            return b

        def down_groups(ei):
            hb = ei % 2
            wdb = swd[hb]
            for sc in range(4):
                n = 128 if sc < 3 else CL - 384
                for nn in range(2):
                    bank = nbank()
                    for kk in range(NF):
                        t.matmul(bank[0:n, :], sh1[hb][:, kk, sc * 128:sc * 128 + n],
                                 wdb[:, kk, nn * 512:(nn + 1) * 512],
                                 start=(kk == 0), stop=(kk == NF - 1))
                    v.tensor_copy(sbo[0:n, nn * 512:(nn + 1) * 512], bank[0:n, :])
                    if nn == 1:
                        q.dma_start(out=BO[ei * CL + sc * 128:ei * CL + sc * 128 + n, :],
                                    in_=sbo[0:n, :])
                    yield

        bucket_load(0, 0)
        for _ in transp_steps(0, 0):
            pass
        dn = iter(())
        for ei in range(E0):
            buf = ei % 2
            wgb, wub, wdb = swg[buf], swu[buf], swd[buf]
            if ei + 1 < E0:
                bucket_load(ei + 1, 1 - buf)
                nxt = transp_steps(ei + 1, 1 - buf)
            else:
                nxt = iter(())
            for m in range(NF):
                bank = nbank()
                for kk in range(NT):
                    t.matmul(bank[:, 0:CL], wgb[:, kk, m * 128:(m + 1) * 128], sbkT[buf][:, kk, :],
                             start=(kk == 0), stop=(kk == NT - 1))
                s.activation(sh1[buf][:, m, :], bank[:, 0:CL], AF.Silu)
                next(dn, None)
            for _ in dn:
                pass
            if 1 < ei + 1 < E0:
                wload(ei + 1, 1 - buf)
            for m in range(NF):
                bank = nbank()
                for kk in range(NT):
                    t.matmul(bank[:, 0:CL], wub[:, kk, m * 128:(m + 1) * 128], sbkT[buf][:, kk, :],
                             start=(kk == 0), stop=(kk == NT - 1))
                v.tensor_mul(sh1[buf][:, m, :], sh1[buf][:, m, :], bank[:, 0:CL])
                if m % 2 == 1:
                    next(nxt, None)
            for _ in nxt:
                pass
            dn = down_groups(ei)
        for _ in dn:
            pass

    gW.close()
    r.fence()
    sxhT = e(nc.sbuf_tensor("sxhT", [128, NT, T], BF16))
    sxlT = e(nc.sbuf_tensor("sxlT", [128, NT, T], BF16))
    with ExitStack() as ph:
        e2 = ph.enter_context
        sG0 = [e2(nc.sbuf_tensor(f"sG0_{i}", [128, 1024], BF16)) for i in range(2)]
        sG1 = [e2(nc.sbuf_tensor(f"sG1_{i}", [128, 1024], BF16)) for i in range(2)]
        sxc = [e2(nc.sbuf_tensor(f"sxc{i}", [128, 1024], F32)) for i in range(2)]
        soc = [e2(nc.sbuf_tensor(f"soc{i}", [128, 1024], BF16)) for i in range(2)]
        sshc = [e2(nc.sbuf_tensor(f"sshc{i}", [128, 1024], BF16)) for i in range(2)]
        sx1 = e2(nc.sbuf_tensor("sx1", [128, 1024], F32))
        sx2 = [e2(nc.sbuf_tensor(f"sx2_{i}", [128, 1024], F32)) for i in range(2)]
        srt = e2(nc.sbuf_tensor("srt", [128, 1024], F32))
        sgg = e2(nc.sbuf_tensor("sgg", [128, 1024], F32))
        shi = e2(nc.sbuf_tensor("shi", [128, 1024], BF16))
        slo = e2(nc.sbuf_tensor("slo", [128, 1024], BF16))
        ssq = e2(nc.sbuf_tensor("ssq", [128, 1024], BF16))
        scw = e2(nc.sbuf_tensor("scw", [128, NC, 4], F32))

        def fetch1(c):
            i = c % 2
            g.indirect_dma_start(sG0[i][:, :], None, BO[:, :], IndirectOffsetOnAxis(sicol[:, c, 0:1], 0))
            g.indirect_dma_start(sG1[i][:, :], None, BO[:, :], IndirectOffsetOnAxis(sicol[:, c, 1:2], 0))
            q.dma_start(out=sxc[i][:, :], in_=x_tok[c * 128:(c + 1) * 128, :])
            q.dma_start(out=soc[i][:, :], in_=o_tok[c * 128:(c + 1) * 128, :])
            q.dma_start(out=sshc[i][:, :], in_=sh_tok[c * 128:(c + 1) * 128, :])

        fetch1(0)
        for c in range(NC):
            i = c % 2
            if c + 1 < NC:
                fetch1(c + 1)
            v.tensor_scalar(srt[:, :], sG0[i][:, :], scol[:, c, 2:3], None, op0=ALU.mult)
            v.scalar_tensor_tensor(srt[:, :], sG1[i][:, :], scol[:, c, 3:4], srt[:, :], ALU.mult, ALU.add)
            v.tensor_scalar(sgg[:, :], sxc[i][:, :], scol[:, c, 5:6], scol[:, c, 4:5], op0=ALU.mult, op1=ALU.add)
            v.tensor_add(srt[:, :], srt[:, :], sgg[:, :])
            v.tensor_tensor(srt[:, :], srt[:, :], sshc[i][:, :], ALU.add)
            s.activation(srt[:, :], srt[:, :], AF.Silu)
            v.tensor_tensor(sgg[:, :], sgg[:, :], soc[i][:, :], ALU.add)
            v.tensor_mul(sgg[:, :], sgg[:, :], srt[:, :])
            v.tensor_add(sx1[:, :], sxc[i][:, :], sgg[:, :])
            q.dma_start(out=x1_o[c * 128:(c + 1) * 128, :], in_=sx1[:, :])
            v.reduce_sum(scw[:, c, 0:1], sx1[:, :], axis=AX.X)
            v.tensor_mul(ssq[:, :], sx1[:, :], sx1[:, :])
            v.reduce_sum(scw[:, c, 1:2], ssq[:, :], axis=AX.X)
            v.tensor_scalar_mul(scw[:, c, 0:1], scw[:, c, 0:1], 1.0 / D)
            v.tensor_scalar_mul(scw[:, c, 1:2], scw[:, c, 1:2], 1.0 / D)
            v.tensor_mul(scw[:, c, 2:3], scw[:, c, 0:1], scw[:, c, 0:1])
            v.tensor_sub(scw[:, c, 1:2], scw[:, c, 1:2], scw[:, c, 2:3])
            v.tensor_scalar_add(scw[:, c, 1:2], scw[:, c, 1:2], EPS)
        s.activation(scw[:, :, 2:3], scw[:, :, 1:2], AF.Sqrt)
        v.reciprocal(scw[:, :, 2:3], scw[:, :, 2:3])
        v.tensor_mul(scw[:, :, 3:4], scw[:, :, 0:1], scw[:, :, 2:3])
        v.tensor_scalar_mul(scw[:, :, 3:4], scw[:, :, 3:4], -1.0)
        g.dma_start(out=sx2[0][:, :], in_=x1_o[0:128, :])
        for c in range(NC):
            i = c % 2
            if c + 1 < NC:
                g.dma_start(out=sx2[1 - i][:, :], in_=x1_o[(c + 1) * 128:(c + 2) * 128, :])
            v.tensor_scalar(sgg[:, :], sx2[i][:, :], scw[:, c, 2:3], scw[:, c, 3:4],
                            op0=ALU.mult, op1=ALU.add)
            v.tensor_copy(shi[:, :], sgg[:, :])
            v.tensor_tensor(srt[:, :], sgg[:, :], shi[:, :], ALU.subtract)
            v.tensor_copy(slo[:, :], srt[:, :])
            for (src, dst) in ((shi, sxhT), (slo, sxlT)):
                for j in range(NT):
                    bank = (PE2 if j % 2 == 0 else PF)[:, :].bitcast(BF16)
                    t.transpose(bank[0:128, 0:128], src[:, j * 128:(j + 1) * 128], sid128b[:, :])
                    v.tensor_copy(dst[:, j, c * 128:(c + 1) * 128], bank[0:128, 0:128])

    r.fence()
    with ExitStack() as ph:
        e3 = ph.enter_context
        stmp0 = e3(nc.sbuf_tensor("stmp0", [128, 1024], F32))
        sWg = e3(nc.sbuf_tensor("sWg", [128, NT, E1], F32))
        sWgh = e3(nc.sbuf_tensor("sWgh", [128, NT, E1], BF16))
        sWgl = e3(nc.sbuf_tensor("sWgl", [128, NT, E1], BF16))
        slog8 = e3(nc.sbuf_tensor("slog8", [E1, T], F32))
        sltm = e3(nc.sbuf_tensor("sltm", [128, NC, E1], F32))
        sw8 = e3(nc.sbuf_tensor("sw8", [128, NC, 12], F32))
        stk8 = e3(nc.sbuf_tensor("stk8", [128, 24], F32))
        szer8 = e3(nc.sbuf_tensor("szer8", [128, E1], F32))
        ohA1 = e3(nc.sbuf_tensor("ohA1", [E1, T], F32))
        ohC1 = e3(nc.sbuf_tensor("ohC1", [E1, T], F32))
        q.dma_start(out=sWg[:], in_=Wg1.rearrange("(a p) f -> p a f", p=128))
        v.tensor_copy(sWgh[:], sWg[:])
        v.tensor_tensor(sWg[:], sWg[:], sWgh[:], ALU.subtract)
        v.tensor_copy(sWgl[:], sWg[:])
        v.memset(szer8[:], 0.0)
        for nn in range(4):
            bank = P0 if nn % 2 == 0 else P1
            first = True
            for (xs, ws) in ((sxhT, sWgh), (sxhT, sWgl), (sxlT, sWgh)):
                for kk in range(NT):
                    t.matmul(bank[0:E1, 0:512], ws[:, kk, :], xs[:, kk, nn * 512:(nn + 1) * 512],
                             start=first, stop=(not first and kk == NT - 1 and ws is sWgh and xs is sxlT))
                    first = False
            v.tensor_copy(slog8[:, nn * 512:(nn + 1) * 512], bank[0:E1, 0:512])
        for c in range(NC):
            bank = P0 if c % 2 == 0 else P1
            t.transpose(bank[0:128, 0:E1], slog8[:, c * 128:(c + 1) * 128], sident8[:, :])
            v.tensor_copy(sltm[:, c, :], bank[0:128, 0:E1])
        for c in range(NC):
            ch = sltm[:, c, :]
            w8 = sw8[:, c, :]
            v.max(out=w8[:, 0:8], in_=ch)
            v.tensor_scalar_mul(w8[:, 8:9], w8[:, 0:1], -1.0)
            s.activation(stk8[:, 0:E1], ch, AF.Exp, bias=w8[:, 8:9], accum_out=w8[:, 9:10])
            v.reciprocal(scol[:, c, 6:7], w8[:, 9:10])
            v.tensor_single_scalar(stk8[:, 0:E1], ch, w8[:, 0:1], ALU.is_ge)
            v.tensor_tensor_scan(stk8[:, 8:16], stk8[:, 0:E1], szer8[:, 0:E1], 0.0, ALU.add, ALU.add)
            v.tensor_single_scalar(stk8[:, 8:16], stk8[:, 8:16], 1.0, ALU.is_equal)
            v.tensor_mul(stk8[:, 0:E1], stk8[:, 0:E1], stk8[:, 8:16])
            v.tensor_mul(stk8[:, 8:16], stk8[:, 0:E1], siota8r[:, :])
            v.reduce_sum(scol[:, c, 7:8], stk8[:, 8:16], axis=AX.X)
        col_to_row(scol[:, :, 7:8], A0)
        q.dma_start(out=rows1_o[0:1, 0:T], in_=A0)
        for hf in range(2):
            r.bcast(PST[0:E1, :], sones[0:1, 0:E1], A0[0:1, hf * 1024:(hf + 1) * 1024])
            v.tensor_scalar(ohA1[:, hf * 1024:(hf + 1) * 1024], PST[0:E1, 0:1024],
                            siota8c[:, :], None, op0=ALU.is_equal)
        v.memset(ohC1[:, :], 0.0)
        v.tensor_tensor_scan(ohC1[:, :], ohA1[:, :], ohC1[:, :], 0.0, ALU.add, ALU.add)
        v.tensor_copy(scnt1[:, :], ohC1[:, T - 1:T])
        q.dma_start(out=cnt1_o[:, :], in_=scnt1[:, :])
        v.tensor_sub(ohC1[:, :], ohC1[:, :], ohA1[:, :])
        v.tensor_mul(ohC1[:, :], ohC1[:, :], ohA1[:, :])
        for nn in range(4):
            bank = P0 if nn % 2 == 0 else P1
            t.matmul(bank[0:1, 0:512], sonescf[0:E1, :], ohC1[:, nn * 512:(nn + 1) * 512],
                     start=True, stop=True)
            v.tensor_copy(A1[0:1, nn * 512:(nn + 1) * 512], bank[0:1, 0:512])
        q.dma_start(out=rows1_o[0:1, T:2 * T], in_=A1)
        sq_a = e3(nc.sbuf_tensor("sq_a", [HD, T], BF16))
        sq_b = e3(nc.sbuf_tensor("sq_b", [HD + 1, T], BF16))
        sk_a = e3(nc.sbuf_tensor("sk_a", [HD, T], BF16))
        sk_b = e3(nc.sbuf_tensor("sk_b", [HD + 1, T], BF16))
        svt1 = e3(nc.sbuf_tensor("svt1", [128, NC, 128], BF16))
        sst1 = e3(nc.sbuf_tensor("sst1", [128, T], BF16))
        sE1b = e3(nc.sbuf_tensor("sE1b", [128, T], BF16))
        swq = e3(nc.sbuf_tensor("swq", [128, NT, 128], BF16))
        shdm = e3(nc.sbuf_tensor("shdm", [128, 1], F32))
        shdmi = e3(nc.sbuf_tensor("shdmi", [128, 1], I32))
        q.dma_start(out=shdmi[:], in_=hdm[:, :])
        v.tensor_copy(shdm[:, :], shdmi[:, :])
        for (W, da, db) in ((wq1, sq_a, sq_b), (wk1, sk_a, sk_b)):
            g.dma_start(out=swq[:], in_=W.rearrange("(a p) f -> p a f", p=128))
            for nn in range(4):
                bank = P0 if nn % 2 == 0 else P1
                for kk in range(NT):
                    t.matmul(bank[0:HD, 0:512], swq[:, kk, 0:HD],
                             sxhT[:, kk, nn * 512:(nn + 1) * 512],
                             start=(kk == 0), stop=(kk == NT - 1))
                v.tensor_copy(da[:, nn * 512:(nn + 1) * 512], bank[0:HD, 0:512])
                bank2 = P1 if nn % 2 == 0 else P0
                for kk in range(NT):
                    t.matmul(bank2[0:HD, 0:512], swq[:, kk, HD:128],
                             sxhT[:, kk, nn * 512:(nn + 1) * 512],
                             start=(kk == 0), stop=(kk == NT - 1))
                v.tensor_copy(db[0:HD, nn * 512:(nn + 1) * 512], bank2[0:HD, 0:512])
        v.memset(sk_b[HD:HD + 1, :], 1.0)
        g.dma_start(out=swq[:], in_=wv1.rearrange("(a p) f -> p a f", p=128))
        for c in range(NC):
            bank = P0 if c % 2 == 0 else P1
            for kk in range(NT):
                t.matmul(bank[:, 0:128], sxhT[:, kk, c * 128:(c + 1) * 128], swq[:, kk, 0:128],
                         start=(kk == 0), stop=(kk == NT - 1))
            v.tensor_copy(svt1[:, c, :], bank[:, 0:128])
        for qc in range(NC):
            for nn in range(4):
                bank = PE2 if nn % 2 == 0 else PF
                t.matmul(bank[:, 0:512], sq_a[:, qc * 128:(qc + 1) * 128],
                         sk_a[:, nn * 512:(nn + 1) * 512], start=True, stop=False)
                t.matmul(bank[:, 0:512], sq_b[0:HD, qc * 128:(qc + 1) * 128],
                         sk_b[0:HD, nn * 512:(nn + 1) * 512], start=False, stop=True)
                v.reduce_max(scol[:, qc, 7:8] if nn == 0 else stmp0[:, nn:nn + 1],
                             bank[:, 0:512], axis=AX.X)
                if nn > 0:
                    v.tensor_max(scol[:, qc, 7:8], scol[:, qc, 7:8], stmp0[:, nn:nn + 1])
        col_to_row(scol[:, :, 7:8], A2)
        v.tensor_scalar_mul(A2, A2, -1.0)
        g.dma_start(out=sq_b[HD:HD + 1, :], in_=A2)
        SC1 = 1.0 / float(np.sqrt(128.0))
        for hf in range(2):
            qs = slice(hf * 1024, (hf + 1) * 1024)
            def sim1_mm(kc):
                bank = PE2 if kc % 2 == 0 else PF
                for nn in range(2):
                    t.matmul(bank[:, nn * 512:(nn + 1) * 512],
                             sk_a[:, kc * 128:(kc + 1) * 128],
                             sq_a[:, hf * 1024 + nn * 512:hf * 1024 + (nn + 1) * 512],
                             start=True, stop=False)
                    t.matmul(bank[:, nn * 512:(nn + 1) * 512],
                             sk_b[:, kc * 128:(kc + 1) * 128],
                             sq_b[:, hf * 1024 + nn * 512:hf * 1024 + (nn + 1) * 512],
                             start=False, stop=True)
            sim1_mm(0)
            for kc in range(NC):
                bank = PE2 if kc % 2 == 0 else PF
                eb = sE1b[:, 0:1024] if kc % 2 == 0 else sE1b[:, 1024:2048]
                s.activation(eb, bank[:, 0:1024], AF.Exp, scale=SC1)
                if kc + 1 < NC:
                    sim1_mm(kc + 1)
                for nn in range(2):
                    t.matmul(PST[:, nn * 512:(nn + 1) * 512], svt1[:, kc, :],
                             eb[:, nn * 512:(nn + 1) * 512],
                             start=(kc == 0), stop=(kc == NC - 1))
                t.matmul(P0[0:1, 0:512], sonesbc[:, :], eb[:, 0:512],
                         start=(kc == 0), stop=(kc == NC - 1))
                t.matmul(P1[0:1, 0:512], sonesbc[:, :], eb[:, 512:1024],
                         start=(kc == 0), stop=(kc == NC - 1))
            v.tensor_copy(A3[0:1, hf * 1024:hf * 1024 + 512], P0[0:1, 0:512])
            v.tensor_copy(A3[0:1, hf * 1024 + 512:(hf + 1) * 1024], P1[0:1, 0:512])
            v.reciprocal(A3[0:1, qs], A3[0:1, qs])
            v.tensor_copy(stmp0[:, 0:1024], PST[:, 0:1024])
            r.bcast(PE2[:, :], sones[:, :], A3[0:1, qs])
            v.tensor_mul(stmp0[:, 0:1024], stmp0[:, 0:1024], PE2[:, 0:1024])
            v.tensor_scalar(sst1[:, qs], stmp0[:, 0:1024], shdm[:, 0:1], None, op0=ALU.mult)
        swo = e3(nc.sbuf_tensor("swo", [128, D], BF16))
        sstg1 = e3(nc.sbuf_tensor("sstg1", [128, 1024], BF16))
        g.dma_start(out=swo[:], in_=wo1[:, :])
        for c in range(NC):
            for nn in range(2):
                bank = P0 if nn % 2 == 0 else P1
                t.matmul(bank[:, :], sst1[:, c * 128:(c + 1) * 128],
                         swo[:, nn * 512:(nn + 1) * 512], start=True, stop=True)
                v.tensor_copy(sstg1[:, nn * 512:(nn + 1) * 512], bank[:, :])
            q.dma_start(out=s1_o[c * 128:(c + 1) * 128, :], in_=sstg1[:, :])
        sWe = e3(nc.sbuf_tensor("sWe", [128, NT, D], BF16))
        sbeb = e3(nc.sbuf_tensor("sbeb", [128, D], F32))
        g.dma_start(out=sWe[:], in_=We1.rearrange("(a p) f -> p a f", p=128))
        q.dma_start(out=sbeb[:], in_=be_b[:, :])
        for c in range(NC):
            for nn in range(2):
                bank = BK6[(c * 2 + nn) % 6]
                for kk in range(NT):
                    t.matmul(bank[:, :], sxhT[:, kk, c * 128:(c + 1) * 128],
                             sWe[:, kk, nn * 512:(nn + 1) * 512],
                             start=(kk == 0), stop=(kk == NT - 1))
                v.tensor_tensor(stmp0[:, nn * 512:(nn + 1) * 512], bank[:, :],
                                sbeb[:, nn * 512:(nn + 1) * 512], ALU.add)
            v.tensor_scalar(sstg1[:, :], stmp0[:, 0:1024], scol[:, c, 6:7], None, op0=ALU.mult)
            q.dma_start(out=z_o[c * 128:(c + 1) * 128, :], in_=sstg1[:, :])
    ctx.close()
    ectx = ExitStack()
    with ectx:
        r.emit(ectx)
    return nc



def build_C():
    nc = bass.Bass("TRN2", target_bir_lowering=False, detect_race_conditions=False)
    dtr = nc.dram_tensor
    I = lambda n, s, d=F32: dtr(n, s, d, kind="ExternalInput").ap()
    O = lambda n, s, d=F32: dtr(n, s, d, kind="ExternalOutput").ap()
    x1 = I("x1", [T, D])
    z_tok = I("z_tok", [T, D], BF16)
    s1_tok = I("s1_tok", [T, D], BF16)
    rows1 = I("rows1", [1, 2 * T])
    cnt1s = I("cnt1s", [8, E1])
    coremask = I("coremask", [8, 1])
    onesr = I("onesr", [1, 128])
    iota8c = I("iota8c", [E1, 1])
    id128 = I("id128", [128, 128])
    y_o = O("y", [T, D])

    r = Rec(nc)
    q, g, v, s, t = r.q, r.g, r.v, r.s, r.t
    ctx = ExitStack()
    e = ctx.enter_context
    ar = e(nc.sbuf_tensor("arena", [1, 4 * T], F32))
    A0 = ar[0:1, 0:T]; A1 = ar[0:1, T:2 * T]; A2 = ar[0:1, 2 * T:3 * T]; A3 = ar[0:1, 3 * T:4 * T]
    sones = e(nc.sbuf_tensor("sones", [1, 128], F32))
    siota8c = e(nc.sbuf_tensor("siota8c", [E1, 1], F32))
    sid128 = e(nc.sbuf_tensor("sid128", [128, 128], F32))
    scm = e(nc.sbuf_tensor("scm", [8, 1], F32))
    scnts = e(nc.sbuf_tensor("scnts", [8, E1], F32))
    soffs = e(nc.sbuf_tensor("soffs", [E1, 1], F32))
    swin = e(nc.sbuf_tensor("swin", [128, NC, 1], F32))
    ohA1 = e(nc.sbuf_tensor("ohA1", [E1, T], F32))
    sz = [e(nc.sbuf_tensor(f"sz{i}", [128, 1024], BF16)) for i in range(2)]
    ss1 = [e(nc.sbuf_tensor(f"ss1{i}", [128, 1024], BF16)) for i in range(2)]
    sx1 = [e(nc.sbuf_tensor(f"sx1{i}", [128, 1024], F32)) for i in range(2)]
    sg1 = [e(nc.sbuf_tensor(f"sg1{i}", [128, 1024], F32)) for i in range(2)]
    P0 = e(nc.psum_tensor("P0", [128, 512], F32))
    P1 = e(nc.psum_tensor("P1", [128, 512], F32))
    PST = e(nc.psum_tensor("PST", [128, 1024], F32))

    q.dma_start(out=sones[:], in_=onesr[:, :])
    q.dma_start(out=siota8c[:], in_=iota8c[:, :])
    q.dma_start(out=sid128[:], in_=id128[:, :])
    q.dma_start(out=scm[:], in_=coremask[:, :])
    q.dma_start(out=scnts[:], in_=cnt1s[:, :])
    t.matmul(P0[0:E1, 0:1], scnts[:, :], scm[:, :], start=True, stop=True)
    v.tensor_copy(soffs[:, :], P0[0:E1, 0:1])
    q.dma_start(out=A0, in_=rows1[0:1, 0:T])
    for hf in range(2):
        r.bcast(PST[0:E1, :], sones[0:1, 0:E1], A0[0:1, hf * 1024:(hf + 1) * 1024])
        v.tensor_scalar(ohA1[:, hf * 1024:(hf + 1) * 1024], PST[0:E1, 0:1024],
                        siota8c[:, :], None, op0=ALU.is_equal)
    for nn in range(4):
        bank = P0 if nn % 2 == 0 else P1
        t.matmul(bank[0:1, 0:512], soffs[:, :], ohA1[:, nn * 512:(nn + 1) * 512],
                 start=True, stop=True)
        v.tensor_copy(A2[0:1, nn * 512:(nn + 1) * 512], bank[0:1, 0:512])
    q.dma_start(out=A1, in_=rows1[0:1, T:2 * T])
    v.tensor_add(A1, A1, A2)
    v.tensor_single_scalar(A1, A1, float(C1), ALU.is_lt)
    for c in range(NC):
        bank = P0 if c % 2 == 0 else P1
        t.transpose(bank[0:128, 0:1], A1[0:1, c * 128:(c + 1) * 128], sid128[0:1, 0:1])
        v.tensor_copy(swin[:, c, :], bank[0:128, 0:1])
    def cfetch(c):
        i = c % 2
        q.dma_start(out=sz[i][:, :], in_=z_tok[c * 128:(c + 1) * 128, :])
        g.dma_start(out=ss1[i][:, :], in_=s1_tok[c * 128:(c + 1) * 128, :])
        q.dma_start(out=sx1[i][:, :], in_=x1[c * 128:(c + 1) * 128, :])

    cfetch(0)
    for c in range(NC):
        i = c % 2
        if c + 1 < NC:
            cfetch(c + 1)
        s.activation(sg1[i][:, :], sz[i][:, :], AF.Silu, scale=swin[:, c, 0:1])
        v.tensor_tensor(sg1[i][:, :], sg1[i][:, :], ss1[i][:, :], ALU.mult)
        v.tensor_add(sg1[i][:, :], sg1[i][:, :], sx1[i][:, :])
        g.dma_start(out=y_o[c * 128:(c + 1) * 128, :], in_=sg1[i][:, :])
    ctx.close()
    ectx = ExitStack()
    with ectx:
        r.emit(ectx)
    return nc



def _consts16():
    import numpy as _np
    return {
        "onesr": _np.ones((1, 128), _np.float32),
        "onescf": _np.ones((128, 1), _np.float32),
        "onesbc": _np.ones((128, 1), ml_dtypes.bfloat16),
        "iota16c": _np.arange(E0, dtype=_np.float32).reshape(E0, 1),
        "iota16r": _np.tile(_np.arange(E0, dtype=_np.float32), (128, 1)),
        "ident16": _np.eye(E0, dtype=_np.float32),
        "id128": _np.eye(128, dtype=_np.float32),
    }


def prep_A(x_b, ins):
    Wo = ins["L0_attn_Wo"]
    WoR = Wo.reshape(NH, HD, D).transpose(1, 0, 2).reshape(HD, NH * D)
    im = {
        "xT": np.ascontiguousarray(x_b.T).astype(np.float32),
        "Wq": ins["L0_attn_Wq"], "Wk": ins["L0_attn_Wk"], "Wv": ins["L0_attn_Wv"],
        "WoR": np.ascontiguousarray(WoR),
        "bqR": np.ascontiguousarray(ins["L0_attn_bq"].reshape(NH, HD).T),
        "bkR": np.ascontiguousarray(ins["L0_attn_bk"].reshape(NH, HD).T),
        "bvB": np.ascontiguousarray(np.broadcast_to(ins["L0_attn_bv"], (128, D))),
        "bo": ins["L0_attn_bo"].reshape(D, 1),
        "lnw": ins["L0_attn_ln_w"].reshape(D, 1),
        "lnb": ins["L0_attn_ln_b"].reshape(D, 1),
        "rw": ins["L0_ffn_router_w"], "rb": ins["L0_ffn_router_b"].reshape(E0, 1),
        "swg": ins["L0_ffn_sw_gate"], "swu": ins["L0_ffn_sw_up"], "swd": ins["L0_ffn_sw_down"],
    }
    im.update(_consts16())
    return im


def prep_B(b, x_b, ra, cnts_all, ins):
    im = {
        "x_tok": np.ascontiguousarray(x_b).astype(np.float32),
        "o_tok": np.ascontiguousarray(np.asarray(ra["oT"]).T),
        "sh_tok": np.ascontiguousarray(np.asarray(ra["shT"]).T),
        "xnbf_tok": np.ascontiguousarray(np.asarray(ra["xnbf"]).T),
        "rows_in": np.asarray(ra["rows"]),
        "cnts": cnts_all,
        "coremask": (np.arange(8) < b).astype(np.float32).reshape(8, 1),
        "wg_h": ins["L0_ffn_w_gate"], "wu_h": ins["L0_ffn_w_up"], "wd_h": ins["L0_ffn_w_down"],
        "wq1": ins["L1_attn_w_q"], "wk1": ins["L1_attn_w_k"], "wv1": ins["L1_attn_w_v"],
        "wo1": ins["L1_attn_w_o"],
        "Wg1": ins["L1_ffn_Wg"], "We1": ins["L1_ffn_We"],
        "be_b": np.ascontiguousarray(np.broadcast_to(ins["L1_ffn_be"], (128, D))).astype(np.float32),
        "hdm": ins["L1_attn_head_dim_mask"].reshape(128, 1).astype(np.int32),
        "iota8c": np.arange(E1, dtype=np.float32).reshape(E1, 1),
        "iota8r": np.tile(np.arange(E1, dtype=np.float32), (128, 1)),
        "ident8": np.eye(E1, dtype=np.float32),
        "id128b": np.eye(128, dtype=ml_dtypes.bfloat16),
    }
    im.update({k: v for k, v in _consts16().items() if k in ("onesr", "onescf", "onesbc", "id128", "iota16c")})
    return im


def prep_C(b, rb_, cnt1_all):
    return {
        "x1": np.asarray(rb_["x1"]),
        "z_tok": np.asarray(rb_["z_tok"]),
        "s1_tok": np.asarray(rb_["s1_tok"]),
        "rows1": np.asarray(rb_["rows1"]),
        "cnt1s": cnt1_all,
        "coremask": (np.arange(8) < b).astype(np.float32).reshape(8, 1),
        "onesr": np.ones((1, 128), np.float32),
        "iota8c": np.arange(E1, dtype=np.float32).reshape(E1, 1),
        "id128": np.eye(128, dtype=np.float32),
    }


def kernel(**inputs):
    ins = {k: np.asarray(v) for k, v in inputs.items()}
    x = ins["L0_attn_hidden_states"]
    ncA = build_A()
    imsA = [prep_A(x[b], ins) for b in range(8)]
    resA = run_bass_kernel_spmd(ncA, imsA, core_ids=list(range(8))).results
    cnts_all = np.stack([np.asarray(resA[b]["cnt"])[:, 0] for b in range(8)]).astype(np.float32)
    ncB = build_B()
    imsB = [prep_B(b, x[b], resA[b], cnts_all, ins) for b in range(8)]
    resB = run_bass_kernel_spmd(ncB, imsB, core_ids=list(range(8))).results
    cnt1_all = np.stack([np.asarray(resB[b]["cnt1"])[:, 0] for b in range(8)]).astype(np.float32)
    ncC = build_C()
    imsC = [prep_C(b, resB[b], cnt1_all) for b in range(8)]
    resC = run_bass_kernel_spmd(ncC, imsC, core_ids=list(range(8))).results
    y = np.stack([np.asarray(resC[b]["y"]) for b in range(8)])
    return y.astype(np.float32)


if __name__ == "__main__":
    nc = build_A()
    print("built A ok")
    nc = build_B()
    print("built B ok")


# version 23
# speedup vs baseline: 1.0121x; 1.0121x over previous
import numpy as np
import ml_dtypes
from contextlib import ExitStack

import concourse.bass as bass
import concourse.mybir as mybir
from concourse.bass_utils import run_bass_kernel_spmd
from concourse.bass_core_rust import IndirectOffsetOnAxis

F32 = mybir.dt.float32
BF16 = mybir.dt.bfloat16
I32 = mybir.dt.int32
AF = mybir.ActivationFunctionType
ALU = mybir.AluOpType
AX = mybir.AxisListType
AP = bass.AP

D = 1024
T = 2048
NH = 16
HD = 64
E0 = 16
C0 = 2560
CL = 448
NSLOT = E0 * CL
F0 = 1408
E1 = 8
C1 = 2048
EPS = 1e-5
NT = D // 128
NC = T // 128
NF = F0 // 128


_RW = {
    "matmul": ((0,), (1, 2)),
    "transpose": ((0,), (1, 2)),
    "activation": ((0,), (1,)),
    "tensor_copy": ((0,), (1,)),
    "tensor_tensor": ((0,), (1, 2)),
    "tensor_add": ((0,), (1, 2)),
    "tensor_sub": ((0,), (1, 2)),
    "tensor_mul": ((0,), (1, 2)),
    "tensor_max": ((0,), (1, 2)),
    "tensor_scalar": ((0,), (1, 2, 3)),
    "tensor_single_scalar": ((0,), (1, 2)),
    "tensor_scalar_add": ((0,), (1, 2)),
    "tensor_scalar_sub": ((0,), (1, 2)),
    "tensor_scalar_mul": ((0,), (1, 2)),
    "tensor_scalar_max": ((0,), (1, 2)),
    "tensor_scalar_min": ((0,), (1, 2)),
    "scalar_tensor_tensor": ((0,), (1, 2, 3)),
    "tensor_tensor_scan": ((0,), (1, 2, 3)),
    "tensor_reduce": ((0,), (1,)),
    "reduce_sum": ((0,), (1,)),
    "reduce_max": ((0,), (1,)),
    "reciprocal": ((0,), (1,)),
    "select": ((0,), (1, 2, 3)),
    "max": ((), ()),
    "memset": ((0,), ()),
    "memzero": ((0,), ()),
    "iota": ((0,), ()),
    "affine_select": ((0,), (1,)),
    "dma_start": ((), ()),
    "indirect_dma_start": ((0, 1), (2, 3)),
}
_DMA = {"dma_start", "indirect_dma_start"}
_WKW = {"out", "accum_out", "out_max", "out_indices", "out_offset"}


def _aps_of(x):
    if isinstance(x, AP):
        return [x]
    if isinstance(x, IndirectOffsetOnAxis):
        return [x.ap]
    return []


def _tn(ap):
    t = ap.tensor
    return getattr(t, "name", None) or str(id(t))


class _EngProxy:
    def __init__(self, rec, eng):
        self._rec = rec
        self._eng = eng

    def __getattr__(self, name):
        def call(*args, **kwargs):
            return self._rec._record(self._eng, name, args, kwargs)
        return call


class Rec:
    ENGS = ("t", "v", "s", "g", "q")

    def __init__(self, nc):
        self.nc = nc
        self.ops = []
        self.w_last = {}
        self.r_last = {}
        for e in self.ENGS:
            setattr(self, e, _EngProxy(self, e))

    def _record(self, eng, name, args, kwargs):
        assert name in _RW, f"unknown method {name}"
        outp, inp = _RW[name]
        reads, writes = [], []
        for i, a in enumerate(args):
            for ap in _aps_of(a):
                (writes if i in outp else reads).append(_tn(ap))
        for kw, val in kwargs.items():
            for ap in _aps_of(val):
                (writes if kw in _WKW else reads).append(_tn(ap))
        idx = len(self.ops)
        deps = set(getattr(self, "_fence", ()))
        sdeps = set()
        is_dma = name in _DMA
        soft = eng == "t" and not is_dma
        for tn in reads:
            for e2, w in self.w_last.get(tn, {}).items():
                if soft and e2 == eng and not self.ops[w]["dma"]:
                    sdeps.add(w)
                else:
                    deps.add(w)
        for tn in writes:
            for e2, w in self.w_last.get(tn, {}).items():
                if soft and e2 == eng and not self.ops[w]["dma"]:
                    sdeps.add(w)
                else:
                    deps.add(w)
            for e2, rr in self.r_last.get(tn, {}).items():
                if soft and e2 == eng and not self.ops[rr]["dma"]:
                    sdeps.add(rr)
                else:
                    deps.add(rr)
        self.ops.append(dict(eng=eng, name=name, args=args, kwargs=kwargs,
                             deps=deps, sdeps=sdeps, dma=name in _DMA, sig=False))
        for tn in writes:
            self.w_last.setdefault(tn, {})[eng] = idx
        for tn in reads:
            self.r_last.setdefault(tn, {})[eng] = idx
        return idx

    def emit(self, ctx):
        nc = self.nc
        for op in self.ops:
            if op["dma"]:
                op["sig"] = True
            for d in op["deps"]:
                self.ops[d]["sig"] = True
        cnt = {e: 0 for e in self.ENGS}
        for op in self.ops:
            if op["sig"]:
                cnt[op["eng"]] += 16 if op["dma"] else 1
                op["semval"] = cnt[op["eng"]]
        totals = dict(cnt)
        sems = {e: ctx.enter_context(nc.semaphore("sem_" + e)) for e in self.ENGS}
        ops = self.ops
        block = ctx.enter_context(nc.Block())

        def runner(eng_name):
            def body(engine):
                ndma = 0
                waited = {}
                for op in ops:
                    if op["eng"] != eng_name:
                        continue
                    best = {}
                    if op["dma"] and ndma > 0:
                        best[eng_name] = 16 * ndma
                    for d in op["deps"]:
                        dop = ops[d]
                        best[dop["eng"]] = max(best.get(dop["eng"], 0), dop["semval"])
                    for e2, vv in best.items():
                        if waited.get(e2, 0) < vv:
                            engine.wait_ge(sems[e2], vv)
                            waited[e2] = vv
                    inst = getattr(engine, op["name"])(*op["args"], **op["kwargs"])
                    if op["sig"]:
                        inst.then_inc(sems[eng_name], 16 if op["dma"] else 1)
                    if op["dma"]:
                        ndma += 1
                if eng_name in ("q", "g") and totals[eng_name] > 0:
                    engine.wait_ge(sems[eng_name], totals[eng_name])
            return body

        block.sync(runner("q"))
        block.vector(runner("v"))
        block.scalar(runner("s"))
        block.gpsimd(runner("g"))
        block.tensor(runner("t"))

    def fence(self):
        last = {}
        for i, op in enumerate(self.ops):
            last[op["eng"]] = i
        self._fence = set(last.values())

    def bcast(self, psum_ap, ones_row, row_ap):
        w = row_ap.shape[-1]
        for o in range(0, w, 512):
            ww = min(512, w - o)
            self.t.matmul(psum_ap[:, o:o + ww], ones_row, row_ap[:, o:o + ww],
                          start=True, stop=True)



def build_A():
    nc = bass.Bass("TRN2", target_bir_lowering=False, detect_race_conditions=False)
    dtr = nc.dram_tensor
    I = lambda n, s, d=F32: dtr(n, s, d, kind="ExternalInput").ap()
    O = lambda n, s, d=F32: dtr(n, s, d, kind="ExternalOutput").ap()

    xT = I("xT", [D, T])
    Wq = I("Wq", [D, D]); Wk = I("Wk", [D, D]); Wv = I("Wv", [D, D])
    WoR = I("WoR", [HD, NH * D])
    bqR = I("bqR", [HD, NH]); bkR = I("bkR", [HD, NH])
    bvB = I("bvB", [128, D])
    bo = I("bo", [D, 1]); lnw = I("lnw", [D, 1]); lnb = I("lnb", [D, 1])
    rw = I("rw", [D, E0]); rb = I("rb", [E0, 1])
    swg = I("swg", [D, F0]); swu = I("swu", [D, F0]); swd = I("swd", [F0, D])
    onesr = I("onesr", [1, 128])
    onescf = I("onescf", [128, 1])
    onesbc = I("onesbc", [128, 1], BF16)
    iota16c = I("iota16c", [E0, 1])
    iota16r = I("iota16r", [128, E0])
    ident16 = I("ident16", [E0, E0])
    id128 = I("id128", [128, 128])

    oT_o = O("oT", [D, T], BF16)
    shT_o = O("shT", [D, T], BF16)
    xnbf_o = O("xnbf", [D, T], BF16)
    rows_o = O("rows", [1, 8 * T])
    cnt_o = O("cnt", [E0, 1])

    r = Rec(nc)
    q, g, v, s, t = r.q, r.g, r.v, r.s, r.t
    ctx = ExitStack()
    e = ctx.enter_context

    ar = e(nc.sbuf_tensor("arena", [1, 4 * T], F32))
    A0 = ar[0:1, 0:T]; A1 = ar[0:1, T:2 * T]; A2 = ar[0:1, 2 * T:3 * T]; A3 = ar[0:1, 3 * T:4 * T]
    stmp0 = e(nc.sbuf_tensor("stmp0", [128, 1024], F32))
    sones = e(nc.sbuf_tensor("sones", [1, 128], F32))
    sonescf = e(nc.sbuf_tensor("sonescf", [128, 1], F32))
    sonesbc = e(nc.sbuf_tensor("sonesbc", [128, 1], BF16))
    siota16c = e(nc.sbuf_tensor("siota16c", [E0, 1], F32))
    siota16r = e(nc.sbuf_tensor("siota16r", [128, E0], F32))
    sident16 = e(nc.sbuf_tensor("sident16", [E0, E0], F32))
    sid128 = e(nc.sbuf_tensor("sid128", [128, 128], F32))
    sbias = e(nc.sbuf_tensor("sbias", [128, NT, 3], F32))
    srb = e(nc.sbuf_tensor("srb", [E0, 1], F32))
    scsum = e(nc.sbuf_tensor("scsum", [E0, 1], F32))
    scnt = e(nc.sbuf_tensor("scnt", [E0, 1], F32))
    szer16 = e(nc.sbuf_tensor("szer16", [128, E0], F32))

    P0 = e(nc.psum_tensor("P0", [128, 512], F32))
    P1 = e(nc.psum_tensor("P1", [128, 512], F32))
    PST = e(nc.psum_tensor("PST", [128, 1024], F32))
    PE2 = e(nc.psum_tensor("PE2", [128, 1024], F32))
    PF = e(nc.psum_tensor("PF", [128, 1024], F32))
    BK6 = [P0[:, 0:512], P1[:, 0:512], PE2[:, 0:512], PE2[:, 512:1024],
           PF[:, 0:512], PF[:, 512:1024]]

    q.dma_start(out=sones[:], in_=onesr[:, :])
    q.dma_start(out=sonescf[:], in_=onescf[:, :])
    q.dma_start(out=sonesbc[:], in_=onesbc[:, :])
    q.dma_start(out=siota16c[:], in_=iota16c[:, :])
    q.dma_start(out=siota16r[:], in_=iota16r[:, :])
    q.dma_start(out=sident16[:], in_=ident16[:, :])
    q.dma_start(out=sid128[:], in_=id128[:, :])
    q.dma_start(out=srb[:], in_=rb[:, :])
    for a in range(NT):
        q.dma_start(out=sbias[:, a, 0:1], in_=lnw[a * 128:(a + 1) * 128, :])
        q.dma_start(out=sbias[:, a, 1:2], in_=lnb[a * 128:(a + 1) * 128, :])
        q.dma_start(out=sbias[:, a, 2:3], in_=bo[a * 128:(a + 1) * 128, :])
    v.memset(szer16[:], 0.0)

    sxnbf_scope = ExitStack()
    sxnbf = sxnbf_scope.enter_context(nc.sbuf_tensor("sxnbf", [128, NT, T], BF16))

    with ExitStack() as ph:
        e1 = ph.enter_context
        slog = e1(nc.sbuf_tensor("slog", [E0, T], F32))
        slogtm = e1(nc.sbuf_tensor("slogtm", [128, NC, E0], F32))
        sw16 = e1(nc.sbuf_tensor("sw16", [128, NC, 12], F32))
        spk = e1(nc.sbuf_tensor("spk", [128, NC, 4], F32))
        stk = e1(nc.sbuf_tensor("stk", [128, 48], F32))
        ohA = e1(nc.sbuf_tensor("ohA", [E0, T], F32))
        ohB = e1(nc.sbuf_tensor("ohB", [E0, T], F32))
        ohC = e1(nc.sbuf_tensor("ohC", [E0, T], F32))
        str4 = e1(nc.sbuf_tensor("str4", [4, T], F32))

        g1 = ExitStack()
        sxTp = [g1.enter_context(nc.sbuf_tensor(f"sxT{i}", [128, 2, T], F32))
                for i in range(4)]
        sq2 = g1.enter_context(nc.sbuf_tensor("sq2", [128, T], BF16))
        srwf = g1.enter_context(nc.sbuf_tensor("srwf", [128, NT, E0], F32))

        def xpl(kk):
            return sxTp[kk // 2][:, kk % 2, :]

        xTr = xT.rearrange("(a p) f -> p a f", p=128)
        for i in range(4):
            q.dma_start(out=sxTp[i][:], in_=xTr[:, 2 * i:2 * i + 2, :])
        q.dma_start(out=srwf[:], in_=rw.rearrange("(a p) f -> p a f", p=128))

        for nn in range(4):
            bank = P0 if nn % 2 == 0 else P1
            for kk in range(NT):
                t.matmul(bank[0:1, 0:512], sonescf[:, :], xpl(kk)[:, nn * 512:(nn + 1) * 512],
                         start=(kk == 0), stop=(kk == NT - 1))
            v.tensor_copy(A0[0:1, nn * 512:(nn + 1) * 512], bank[0:1, 0:512])
        x2banks = [P0[0:1, 0:512], P1[0:1, 0:512], PST[0:1, 0:512], PST[0:1, 512:1024]]
        for kk in range(NT):
            v.tensor_mul(sq2[:, :], xpl(kk)[:, :], xpl(kk)[:, :])
            for nn in range(4):
                t.matmul(x2banks[nn], sonesbc[:, :], sq2[:, nn * 512:(nn + 1) * 512],
                         start=(kk == 0), stop=(kk == NT - 1))
                if kk == NT - 1:
                    v.tensor_copy(A1[0:1, nn * 512:(nn + 1) * 512], x2banks[nn])

        v.tensor_scalar_mul(A0, A0, 1.0 / D)
        v.tensor_scalar_mul(A1, A1, 1.0 / D)
        v.tensor_mul(A3, A0, A0)
        v.tensor_sub(A1, A1, A3)
        v.tensor_scalar_add(A3, A1, EPS)
        s.activation(A2, A3, AF.Sqrt)
        v.reciprocal(A2, A2)
        q.dma_start(out=rows_o[0:1, 6 * T:7 * T], in_=A0)
        q.dma_start(out=rows_o[0:1, 7 * T:8 * T], in_=A2)

        v.tensor_mul(A3, A0, A2)
        v.tensor_scalar_mul(A3, A3, -1.0)
        for hf in range(2):
            sl = slice(hf * 1024, (hf + 1) * 1024)
            r.bcast(PST[:, :], sones[:, :], A2[0:1, sl])
            r.bcast(PE2[:, :], sones[:, :], A3[0:1, sl])
            for a in range(NT):
                v.tensor_mul(stmp0[:, 0:1024], xpl(a)[:, sl], PST[:, 0:1024])
                v.tensor_add(sxnbf[:, a, sl], stmp0[:, 0:1024], PE2[:, 0:1024])
        q.dma_start(out=xnbf_o.rearrange("(a p) f -> p a f", p=128), in_=sxnbf[:])

        for kk in range(NT):
            t.matmul(P0[0:E0, 0:1], srwf[:, kk, :], sonescf[:, :],
                     start=(kk == 0), stop=(kk == NT - 1))
        v.tensor_copy(scsum[:, :], P0[0:E0, 0:1])
        for nn in range(4):
            bank = P0 if nn % 2 == 0 else P1
            for kk in range(NT):
                t.matmul(bank[0:E0, 0:512], srwf[:, kk, :], xpl(kk)[:, nn * 512:(nn + 1) * 512],
                         start=(kk == 0), stop=(kk == NT - 1))
            v.tensor_copy(slog[:, nn * 512:(nn + 1) * 512], bank[0:E0, 0:512])
        for hf in range(2):
            sl = slog[:, hf * 1024:(hf + 1) * 1024]
            r.bcast(PST[0:E0, :], sones[0:1, 0:E0], A2[0:1, hf * 1024:(hf + 1) * 1024])
            v.tensor_mul(sl, sl, PST[0:E0, 0:1024])
            r.bcast(PE2[0:E0, :], sones[0:1, 0:E0], A3[0:1, hf * 1024:(hf + 1) * 1024])
            v.scalar_tensor_tensor(sl, PE2[0:E0, 0:1024], scsum[:, :], sl, ALU.mult, ALU.add)
            v.tensor_scalar_add(sl, sl, srb[:, :])

        v.tensor_scalar_add(A3, A1, EPS)
        v.reciprocal(A3, A3)
        v.tensor_mul(A1, A1, A3)
        v.tensor_scalar_add(A1, A1, EPS)
        s.activation(A1, A1, AF.Sqrt)
        v.reciprocal(A1, A1)

        for c in range(NC):
            bank = P0 if c % 2 == 0 else P1
            t.transpose(bank[0:128, 0:E0], slog[:, c * 128:(c + 1) * 128], sident16[:, :])
            v.tensor_copy(slogtm[:, c, :], bank[0:128, 0:E0])
        g1.close()
        r.fence()

        def p_top2a(c):
            def run():
                ch = slogtm[:, c, :]
                w16 = sw16[:, c, :]
                wk = stk
                v.max(out=w16[:, 0:8], in_=ch)
                v.tensor_scalar_mul(w16[:, 8:9], w16[:, 0:1], -1.0)
                v.tensor_single_scalar(wk[:, 0:16], ch, w16[:, 0:1], ALU.is_ge)
                v.tensor_tensor_scan(wk[:, 16:32], wk[:, 0:16], szer16[:, 0:E0], 0.0, ALU.add, ALU.add)
                v.tensor_single_scalar(wk[:, 16:32], wk[:, 16:32], 1.0, ALU.is_equal)
                v.tensor_mul(wk[:, 0:16], wk[:, 0:16], wk[:, 16:32])
                v.tensor_mul(wk[:, 32:48], wk[:, 0:16], siota16r[:, :])
                v.reduce_sum(spk[:, c, 0:1], wk[:, 32:48], axis=AX.X)
                v.tensor_single_scalar(wk[:, 16:32], ch, w16[:, 1:2], ALU.is_ge)
                v.tensor_tensor_scan(wk[:, 32:48], wk[:, 16:32], szer16[:, 0:E0], 0.0, ALU.add, ALU.add)
                v.tensor_single_scalar(wk[:, 32:48], wk[:, 32:48], 2.0, ALU.is_le)
                v.tensor_mul(wk[:, 16:32], wk[:, 16:32], wk[:, 32:48])
                v.tensor_sub(wk[:, 16:32], wk[:, 16:32], wk[:, 0:16])
                v.tensor_mul(wk[:, 16:32], wk[:, 16:32], siota16r[:, :])
                v.reduce_sum(spk[:, c, 1:2], wk[:, 16:32], axis=AX.X)
            return run

        def p_exp(c):
            def run():
                ch = slogtm[:, c, :]
                w16 = sw16[:, c, :]
                s.activation(stk[:, 0:E0], ch, AF.Exp, bias=w16[:, 8:9], accum_out=w16[:, 9:10])
                s.activation(w16[:, 10:11], w16[:, 1:2], AF.Exp, bias=w16[:, 8:9])
                v.tensor_scalar_mul(w16[:, 11:12], w16[:, 9:10], 1e-9)
                v.tensor_add(w16[:, 11:12], w16[:, 11:12], w16[:, 10:11])
                v.tensor_scalar_add(w16[:, 11:12], w16[:, 11:12], 1.0)
                v.reciprocal(spk[:, c, 2:3], w16[:, 11:12])
                v.tensor_mul(spk[:, c, 3:4], w16[:, 10:11], spk[:, c, 2:3])
            return run

        def p_pack():
            for c in range(NC):
                bank = P0 if c % 2 == 0 else P1
                t.transpose(bank[0:4, 0:128], spk[:, c, :], sid128[:, :])
                v.tensor_copy(str4[0:4, c * 128:(c + 1) * 128], bank[0:4, 0:128])
            q.dma_start(out=rows_o[0:1, 0:4 * T].rearrange("a (x f) -> a x f", x=4),
                        in_=str4[0:4, :].unsqueeze(1))
            q.dma_start(out=ohC[0:1, 0:T], in_=str4[1:2, :])

        def p_onehot():
            for (dst, src) in ((ohA, str4[0:1, :]), (ohB, ohC[0:1, 0:T])):
                for hf in range(2):
                    for (bk, o) in ((P0, 0), (P1, 512)):
                        t.matmul(bk[0:E0, 0:512], sones[0:1, 0:E0],
                                 src[0:1, hf * 1024 + o:hf * 1024 + o + 512],
                                 start=True, stop=True)
                        v.tensor_scalar(dst[:, hf * 1024 + o:hf * 1024 + o + 512],
                                        bk[0:E0, 0:512], siota16c[:, :], None, op0=ALU.is_equal)

        def p_scan():
            v.tensor_add(slog[:, :], ohA[:, :], ohB[:, :])
            v.memset(ohC[:, :], 0.0)
            v.tensor_tensor_scan(ohC[:, :], slog[:, :], ohC[:, :], 0.0, ALU.add, ALU.add)
            v.tensor_copy(scnt[:, :], ohC[:, T - 1:T])
            q.dma_start(out=cnt_o[:, :], in_=scnt[:, :])
            v.tensor_sub(slog[:, :], ohC[:, :], slog[:, :])
            v.tensor_sub(ohC[:, :], ohC[:, :], ohB[:, :])
            v.tensor_mul(slog[:, :], slog[:, :], ohA[:, :])
            v.tensor_mul(ohC[:, :], ohC[:, :], ohB[:, :])

        def p_extract(roff, src):
            def run():
                for nn in range(4):
                    bank = P0 if nn % 2 == 0 else P1
                    t.matmul(bank[0:1, 0:512], sonescf[0:E0, :], src[:, nn * 512:(nn + 1) * 512],
                             start=True, stop=True)
                    v.tensor_copy(A3[0:1, nn * 512:(nn + 1) * 512], bank[0:1, 0:512])
                q.dma_start(out=rows_o[0:1, roff * T:(roff + 1) * T], in_=A3)
                v.memset(A3[0:1, 0:1], 0.0)
            return run

        pieces_v = [p_top2a(c) for c in range(NC)]
        pieces_s = [(c + 1, p_exp(c)) for c in range(NC)]
        pieces_s += [(NC, p_pack), (NC, p_onehot), (NC, p_scan),
                     (NC, p_extract(4, slog)), (NC, p_extract(5, ohC))]

        sswg = e1(nc.sbuf_tensor("sswg", [128, NT, F0], BF16))
        sswu = e1(nc.sbuf_tensor("sswu", [128, NT, F0], BF16))
        sswd = e1(nc.sbuf_tensor("sswd", [128, NF, D], BF16))
        sh1 = e1(nc.sbuf_tensor("sh1", [128, NF, 512], BF16))
        sstg = e1(nc.sbuf_tensor("sstg", [128, 2, 512], BF16))
        g.dma_start(out=sswg[:], in_=swg.rearrange("(a p) f -> p a f", p=128))
        g.dma_start(out=sswu[:], in_=swu.rearrange("(a p) f -> p a f", p=128))
        g.dma_start(out=sswd[:], in_=swd.rearrange("(a p) f -> p a f", p=128))
        BKS = [PST[:, 0:512], PST[:, 512:1024], PE2[:, 0:512], PE2[:, 512:1024],
               PF[:, 0:512], PF[:, 512:1024]]
        ngrp = 0

        def piece_tick(in_silu):
            nonlocal ngrp
            ngrp += 1
            if ngrp % 2 != 0:
                return
            done_a = NC - len(pieces_v)
            if not in_silu and pieces_s and pieces_s[0][0] <= done_a:
                pieces_s.pop(0)[1]()
            elif pieces_v:
                pieces_v.pop(0)()

        for nn in range(4):
            tsl = slice(nn * 512, (nn + 1) * 512)
            for m in range(NF):
                bank = BKS[m % 6]
                for kk in range(NT):
                    t.matmul(bank[:, :], sswg[:, kk, m * 128:(m + 1) * 128], sxnbf[:, kk, tsl],
                             start=(kk == 0), stop=(kk == NT - 1))
                s.activation(sh1[:, m, :], bank[:, :], AF.Silu)
                piece_tick(True)
            for m in range(NF):
                bank = BKS[m % 6]
                for kk in range(NT):
                    t.matmul(bank[:, :], sswu[:, kk, m * 128:(m + 1) * 128], sxnbf[:, kk, tsl],
                             start=(kk == 0), stop=(kk == NT - 1))
                v.tensor_mul(sh1[:, m, :], sh1[:, m, :], bank[:, :])
                piece_tick(False)
            for m in range(NT):
                bank = BKS[m % 6]
                stg = sstg[:, m % 2, :]
                for kk in range(NF):
                    t.matmul(bank[:, :], sswd[:, kk, m * 128:(m + 1) * 128], sh1[:, kk, :],
                             start=(kk == 0), stop=(kk == NF - 1))
                v.tensor_copy(stg, bank[:, :])
                q.dma_start(out=shT_o[m * 128:(m + 1) * 128, tsl], in_=stg)
                piece_tick(False)
        for pc in pieces_v:
            pc()
        for _, pc in pieces_s:
            pc()

    r.fence()
    sxq_guard = nc.sbuf_tensor("sxq", [128, NT, T], BF16, side="right")
    sxq = sxq_guard.__enter__()
    for hf in range(2):
        sl = slice(hf * 1024, (hf + 1) * 1024)
        r.bcast(PST[:, :], sones[:, :], A1[0:1, sl])
        for a in range(NT):
            v.tensor_mul(stmp0[:, 0:1024], sxnbf[:, a, sl], PST[:, 0:1024])
            v.tensor_scalar(sxq[:, a, sl], stmp0[:, 0:1024], sbias[:, a, 0:1],
                            sbias[:, a, 1:2], op0=ALU.mult, op1=ALU.add)
    sxnbf_scope.close()
    snegM = e(nc.sbuf_tensor("snegM", [128, NH], F32))

    r.fence()
    svtok = lctx1 = nc.sbuf_tensor("svtok", [128, NC, NH * (HD + 1)], BF16, side="left")
    svtok = lctx1.__enter__()
    lctx2 = nc.sbuf_tensor("skpk", [128, NT, T], BF16, side="left")
    skpk = lctx2.__enter__()
    lctx3 = nc.sbuf_tensor("sqpk", [128, NT, T], BF16, side="left")
    sqpk = lctx3.__enter__()
    with ExitStack() as ph:
        e4 = ph.enter_context
        sWa = e4(nc.sbuf_tensor("sWa", [128, NT, D], BF16))
        sspr = e4(nc.sbuf_tensor("sspr", [16, 128], F32))
        sbqR = e4(nc.sbuf_tensor("sbqR", [HD, NH], F32))
        sbkR = e4(nc.sbuf_tensor("sbkR", [HD, NH], F32))
        sbvB = e4(nc.sbuf_tensor("sbvB", [128, D], F32))
        shsc = e4(nc.sbuf_tensor("shsc", [HD, T], BF16))
        sqsq = e4(nc.sbuf_tensor("sqsq", [HD, T], BF16))
        q.dma_start(out=sbqR[:], in_=bqR[:, :])
        q.dma_start(out=sbkR[:], in_=bkR[:, :])
        q.dma_start(out=sbvB[:], in_=bvB[:, :])
        for (W, bR, dstpk) in ((Wq, sbqR, sqpk), (Wk, sbkR, skpk)):
            g.dma_start(out=sWa[:], in_=W.rearrange("(a p) f -> p a f", p=128))
            for h in range(NH):
                for nn in range(4):
                    bank = BK6[(h * 4 + nn) % 6]
                    for kk in range(NT):
                        t.matmul(bank[0:HD, 0:512], sWa[:, kk, h * HD:(h + 1) * HD],
                                 sxq[:, kk, nn * 512:(nn + 1) * 512],
                                 start=(kk == 0), stop=(kk == NT - 1))
                    v.tensor_scalar(shsc[:, nn * 512:(nn + 1) * 512], bank[0:HD, 0:512],
                                    bR[:, h:h + 1], None, op0=ALU.add)
                s.activation(sqsq[:, :], shsc[:, :], AF.Square)
                for nn in range(4):
                    bank = P0 if nn % 2 == 0 else P1
                    t.matmul(bank[0:1, 0:512], sonesbc[0:HD, :],
                             sqsq[:, nn * 512:(nn + 1) * 512], start=True, stop=True)
                    v.tensor_copy(A3[0:1, nn * 512:(nn + 1) * 512], bank[0:1, 0:512])
                g.dma_start(out=sspr[:, :], in_=A3)
                s.activation(sspr[:, :], sspr[:, :], AF.Sqrt)
                v.tensor_scalar_max(sspr[:, :], sspr[:, :], 1e-12)
                v.reciprocal(sspr[:, :], sspr[:, :])
                g.dma_start(out=A3, in_=sspr[:, :])
                for hf in range(2):
                    sl = slice(hf * 1024, (hf + 1) * 1024)
                    r.bcast(PST[0:HD, :], sones[0:1, 0:HD], A3[0:1, sl])
                    v.tensor_mul(shsc[:, sl], shsc[:, sl], PST[0:HD, 0:1024])
                g.dma_start(out=dstpk[(h % 2) * HD:(h % 2 + 1) * HD, h // 2, :], in_=shsc[:, :])
        g.dma_start(out=sWa[:], in_=Wv.rearrange("(a p) f -> p a f", p=128))
        v.memset(svtok[:].rearrange("p c (h j) -> p c h j", j=HD + 1)[:, :, :, HD:HD + 1], 1.0)
        for c in range(NC):
            for nn in range(2):
                bank = BK6[(c * 2 + nn) % 6]
                for kk in range(NT):
                    t.matmul(bank[:, :], sxq[:, kk, c * 128:(c + 1) * 128],
                             sWa[:, kk, nn * 512:(nn + 1) * 512],
                             start=(kk == 0), stop=(kk == NT - 1))
                v.tensor_add(svtok[:, c, :].rearrange("p (h j) -> p h j", j=HD + 1)[:, nn * 8:(nn + 1) * 8, 0:HD],
                             bank[:, :].rearrange("p (h j) -> p h j", j=HD),
                             sbvB[:, nn * 512:(nn + 1) * 512].rearrange("p (h j) -> p h j", j=HD))
    sxq_guard.__exit__(None, None, None)

    r.fence()
    def attn_step(step, srcpk, dstpk, negM=None, mknegM=None):
        with ExitStack() as ph:
            e5 = ph.enter_context
            sE = [e5(nc.sbuf_tensor(f"sE{step}_{j}", [128, T], BF16)) for j in range(2)]
            sAev = [e5(nc.sbuf_tensor(f"sAev{step}_{j}", [HD, T], BF16)) for j in range(2)]
            sspr5 = [e5(nc.sbuf_tensor(f"sspr{step}_{j}", [8, 128], F32)) for j in range(2)]
            srowb = [e5(nc.sbuf_tensor(f"srowb{step}_{j}", [2, T], BF16)) for j in range(2)]
            shi = [e5(nc.sbuf_tensor(f"shi{step}_{j}", [8, 128], BF16)) for j in range(2)]
            slo = [e5(nc.sbuf_tensor(f"slo{step}_{j}", [8, 128], BF16)) for j in range(2)]
            sden = [e5(nc.sbuf_tensor(f"sden{step}_{j}", [65, 1024], F32)) for j in range(2)]
            sones2b = e5(nc.sbuf_tensor(f"sones2b{step}", [2, HD], BF16))
            v.memset(sones2b[:], 1.0)
            PSIM = [PE2, PF]

            def make_tail(p, pp):
                def tail():
                    for j in range(2):
                        h = 2 * p + j
                        for hf in range(2):
                            qs = slice(hf * 1024, (hf + 1) * 1024)
                            for o in (0, 512):
                                t.matmul(PSIM[j][0:HD, o:o + 512], sones2b[:, :],
                                         srowb[j][:, hf * 1024 + o:hf * 1024 + o + 512],
                                         start=True, stop=True)
                            v.tensor_mul(sAev[j][:, qs], sAev[j][:, qs], PSIM[j][0:HD, 0:1024])
                        if mknegM is not None:
                            v.tensor_mul(sE[j][0:HD, :], sAev[j][:, :], sAev[j][:, :])
                            regs = [PSIM[1 - j][0:1, 0:512], PSIM[1 - j][0:1, 512:1024],
                                    PSIM[j][0:1, 0:512], PSIM[j][0:1, 512:1024]]
                            for nn in range(4):
                                t.matmul(regs[nn], sonesbc[0:HD, :],
                                         sE[j][0:HD, nn * 512:(nn + 1) * 512],
                                         start=True, stop=True)
                            for nn in range(4):
                                v.reduce_max(A3[0:1, 80 + 4 * j + nn:81 + 4 * j + nn],
                                             regs[nn], axis=AX.X)
                            v.reduce_max(A3[0:1, 64 + h:65 + h],
                                         A3[0:1, 80 + 4 * j:84 + 4 * j], axis=AX.X)
                        (q if j == 0 else g).dma_start(
                            out=dstpk[(h % 2) * HD:(h % 2 + 1) * HD, h // 2, :],
                            in_=sAev[j][:, :])
                return tail

            tail = None
            for p in range(NH // 2):
                pp = p % 2
                for hf in range(2):
                    qs = slice(hf * 1024, (hf + 1) * 1024)

                    def sim_mm(j, kc):
                        base = j * HD
                        for nn in range(2):
                            t.matmul(PSIM[j][:, nn * 512:(nn + 1) * 512],
                                     skpk[base:base + HD, p, kc * 128:(kc + 1) * 128],
                                     srcpk[base:base + HD, p,
                                           hf * 1024 + nn * 512:hf * 1024 + (nn + 1) * 512],
                                     start=True, stop=True, tile_position=(base, 0))
                    sim_mm(0, 0)
                    sim_mm(1, 0)
                    for kc in range(NC):
                        for j in range(2):
                            h = 2 * p + j
                            eb = sE[j][:, 0:1024] if kc % 2 == 0 else sE[j][:, 1024:2048]
                            if negM is None:
                                s.activation(eb, PSIM[j][:, 0:1024], AF.Exp)
                            else:
                                s.activation(eb, PSIM[j][:, 0:1024], AF.Exp, bias=negM[:, h:h + 1])
                            if kc + 1 < NC:
                                sim_mm(j, kc + 1)
                            for nn in range(2):
                                pav = (PST[0:65, nn * 512:(nn + 1) * 512] if j == 0
                                       else (P0 if nn == 0 else P1)[0:65, 0:512])
                                t.matmul(pav, svtok[:, kc, h * (HD + 1):(h + 1) * (HD + 1)],
                                         eb[:, nn * 512:(nn + 1) * 512],
                                         start=(kc == 0), stop=(kc == NC - 1))
                            if kc + 1 < NC:
                                base = j * HD
                                t.matmul(PSIM[j][:, 0:512],
                                         skpk[base:base + HD, p, (kc + 1) * 128:(kc + 2) * 128],
                                         srcpk[base:base + HD, p, hf * 1024:hf * 1024 + 512],
                                         start=True, stop=True, tile_position=(base, 0))
                    if hf == 0 and tail is not None:
                        tail()
                        tail = None
                    v.tensor_copy(sAev[0][:, qs], PST[0:HD, 0:1024])
                    s.activation(sAev[1][:, hf * 1024:hf * 1024 + 512], P0[0:HD, 0:512], AF.Copy, scale=1.0)
                    s.activation(sAev[1][:, hf * 1024 + 512:hf * 1024 + 1024], P1[0:HD, 0:512], AF.Copy, scale=1.0)
                    s.activation(sden[0][64:65, 0:1024], PST[64:65, 0:1024], AF.Copy, scale=1.0)
                    v.tensor_copy(sden[1][64:65, 0:512], P0[64:65, 0:512])
                    v.tensor_copy(sden[1][64:65, 512:1024], P1[64:65, 0:512])
                    g.dma_start(out=sspr5[0][:, :], in_=sden[0][64:65, 0:1024])
                    q.dma_start(out=sspr5[1][:, :], in_=sden[1][64:65, 0:1024])
                    for j, eng in ((0, g), (1, q)):
                        v.reciprocal(sspr5[j][:, :], sspr5[j][:, :])
                        v.tensor_copy(shi[j][:, :], sspr5[j][:, :])
                        v.tensor_sub(sspr5[j][:, :], sspr5[j][:, :], shi[j][:, :])
                        v.tensor_copy(slo[j][:, :], sspr5[j][:, :])
                        eng.dma_start(out=srowb[j][0:1, qs], in_=shi[j][:, :])
                        eng.dma_start(out=srowb[j][1:2, qs], in_=slo[j][:, :])
                tail = make_tail(p, pp)
            tail()
            if mknegM is not None:
                s.activation(A3[0:1, 64:64 + NH], A3[0:1, 64:64 + NH], AF.Sqrt)
                v.tensor_scalar_mul(A3[0:1, 64:64 + NH], A3[0:1, 64:64 + NH], -1.0)
                t.matmul(PSIM[0][0:128, 0:NH], sones[0:1, :], A3[0:1, 64:64 + NH],
                         start=True, stop=True)
                v.tensor_copy(mknegM[:, :], PSIM[0][0:128, 0:NH])

    attn_step(0, sqpk, sqpk, negM=None, mknegM=snegM)
    r.fence()
    attn_step(1, sqpk, sqpk, negM=snegM, mknegM=None)

    r.fence()
    with ExitStack() as ph:
        e6 = ph.enter_context
        sWoR = e6(nc.sbuf_tensor("sWoR", [HD, NH * D], BF16))
        o_acc = e6(nc.sbuf_tensor("o_acc", [128, NT, T], BF16))
        scr0 = e6(nc.sbuf_tensor("scr0", [HD, T], BF16))
        scr1 = e6(nc.sbuf_tensor("scr1", [HD, T], BF16))
        g.dma_start(out=sWoR[:], in_=WoR[:, :])
        for grp in range(4):
            heads = [4 * grp + i for i in range(4)]
            up0, up1 = heads[1], heads[3]
            g.dma_start(out=scr0[:, :], in_=sqpk[HD:128, up0 // 2, :])
            g.dma_start(out=scr1[:, :], in_=sqpk[HD:128, up1 // 2, :])
            srcs = {heads[0]: sqpk[0:HD, heads[0] // 2, :],
                    heads[1]: scr0[:, :],
                    heads[2]: sqpk[0:HD, heads[2] // 2, :],
                    heads[3]: scr1[:, :]}
            for m in range(NT):
                for nn in range(4):
                    bank = BK6[(m * 4 + nn) % 6]
                    for ih, h in enumerate(heads):
                        t.matmul(bank[:, :], sWoR[:, h * D + m * 128:h * D + (m + 1) * 128],
                                 srcs[h][:, nn * 512:(nn + 1) * 512],
                                 start=(ih == 0), stop=(ih == 3))
                    if grp == 0:
                        v.tensor_scalar(o_acc[:, m, nn * 512:(nn + 1) * 512], bank[:, :],
                                        sbias[:, m, 2:3], None, op0=ALU.add)
                    else:
                        v.tensor_add(o_acc[:, m, nn * 512:(nn + 1) * 512],
                                     o_acc[:, m, nn * 512:(nn + 1) * 512], bank[:, :])
                if grp == 3:
                    q.dma_start(out=oT_o.rearrange("(a p) f -> p a f", p=128)[:, m, :],
                                in_=o_acc[:, m, :])

    lctx3.__exit__(None, None, None)
    lctx2.__exit__(None, None, None)
    lctx1.__exit__(None, None, None)
    ctx.close()
    ectx = ExitStack()
    with ectx:
        r.emit(ectx)
    return nc





def build_B():
    nc = bass.Bass("TRN2", target_bir_lowering=False, detect_race_conditions=False)
    dtr = nc.dram_tensor
    I = lambda n, s, d=F32: dtr(n, s, d, kind="ExternalInput").ap()
    O = lambda n, s, d=F32: dtr(n, s, d, kind="ExternalOutput").ap()

    x_tok = I("x_tok", [T, D])
    o_tok = I("o_tok", [T, D], BF16)
    sh_tok = I("sh_tok", [T, D], BF16)
    xnbf_tok = I("xnbf_tok", [T, D], BF16)
    rows_in = I("rows_in", [1, 8 * T])
    cnts = I("cnts", [8, E0])
    coremask = I("coremask", [8, 1])
    wg_h = I("wg_h", [E0, D, F0]); wu_h = I("wu_h", [E0, D, F0]); wd_h = I("wd_h", [E0, F0, D])
    wq1 = I("wq1", [D, 128]); wk1 = I("wk1", [D, 128]); wv1 = I("wv1", [D, 128])
    wo1 = I("wo1", [128, D])
    Wg1 = I("Wg1", [D, E1])
    We1 = I("We1", [D, D])
    be_b = I("be_b", [128, D])
    hdm = I("hdm", [128, 1], I32)
    onesr = I("onesr", [1, 128])
    onescf = I("onescf", [128, 1])
    onesbc = I("onesbc", [128, 1], BF16)
    iota8c = I("iota8c", [E1, 1])
    iota8r = I("iota8r", [128, E1])
    ident8 = I("ident8", [E1, E1])
    id128 = I("id128", [128, 128])
    id128b = I("id128b", [128, 128], BF16)
    iota16c = I("iota16c", [E0, 1])

    x1_o = O("x1", [T, D])
    z_o = O("z_tok", [T, D], BF16)
    s1_o = O("s1_tok", [T, D], BF16)
    rows1_o = O("rows1", [1, 2 * T])
    cnt1_o = O("cnt1", [E1, 1])

    bkt = dtr("bkt", [NSLOT + 1, D], BF16, kind="Internal").ap()
    BO = dtr("BO", [NSLOT + 1, D], BF16, kind="Internal").ap()
    xh_h = dtr("xh_h", [T, D], BF16, kind="Internal").ap()
    xl_h = dtr("xl_h", [T, D], BF16, kind="Internal").ap()

    r = Rec(nc)
    q, g, v, s, t = r.q, r.g, r.v, r.s, r.t
    ctx = ExitStack()
    e = ctx.enter_context

    ar = e(nc.sbuf_tensor("arena", [1, 4 * T], F32))
    A0 = ar[0:1, 0:T]; A1 = ar[0:1, T:2 * T]; A2 = ar[0:1, 2 * T:3 * T]; A3 = ar[0:1, 3 * T:4 * T]
    sones = e(nc.sbuf_tensor("sones", [1, 128], F32))
    sonescf = e(nc.sbuf_tensor("sonescf", [128, 1], F32))
    sonesbc = e(nc.sbuf_tensor("sonesbc", [128, 1], BF16))
    siota8c = e(nc.sbuf_tensor("siota8c", [E1, 1], F32))
    siota8r = e(nc.sbuf_tensor("siota8r", [128, E1], F32))
    sident8 = e(nc.sbuf_tensor("sident8", [E1, E1], F32))
    sid128 = e(nc.sbuf_tensor("sid128", [128, 128], F32))
    sid128b = e(nc.sbuf_tensor("sid128b", [128, 128], BF16))
    siota16c = e(nc.sbuf_tensor("siota16c", [E0, 1], F32))
    scol = e(nc.sbuf_tensor("scol", [128, NC, 8], F32))
    sicol = e(nc.sbuf_tensor("sicol", [128, NC, 2], I32))
    scnt1 = e(nc.sbuf_tensor("scnt1", [E1, 1], F32))

    P0 = e(nc.psum_tensor("P0", [128, 512], F32))
    P1 = e(nc.psum_tensor("P1", [128, 512], F32))
    PST = e(nc.psum_tensor("PST", [128, 1024], F32))
    PE2 = e(nc.psum_tensor("PE2", [128, 1024], F32))
    PF = e(nc.psum_tensor("PF", [128, 1024], F32))
    BK6 = [P0[:, 0:512], P1[:, 0:512], PE2[:, 0:512], PE2[:, 512:1024],
           PF[:, 0:512], PF[:, 512:1024]]

    q.dma_start(out=sones[:], in_=onesr[:, :])
    q.dma_start(out=sonescf[:], in_=onescf[:, :])
    q.dma_start(out=sonesbc[:], in_=onesbc[:, :])
    q.dma_start(out=siota8c[:], in_=iota8c[:, :])
    q.dma_start(out=siota8r[:], in_=iota8r[:, :])
    q.dma_start(out=sident8[:], in_=ident8[:, :])
    q.dma_start(out=sid128[:], in_=id128[:, :])
    q.dma_start(out=sid128b[:], in_=id128b[:, :])
    q.dma_start(out=siota16c[:], in_=iota16c[:, :])

    def row_to_col(row, colap):
        for c in range(NC):
            bank = P0 if c % 2 == 0 else P1
            t.transpose(bank[0:128, 0:1], row[0:1, c * 128:(c + 1) * 128], sid128[0:1, 0:1])
            v.tensor_copy(colap[:, c, :], bank[0:128, 0:1])

    def col_to_row(colap, row):
        for c in range(NC):
            bank = P0 if c % 2 == 0 else P1
            t.transpose(bank[0:1, 0:128], colap[:, c, :], sid128[:, :])
            v.tensor_copy(row[0:1, c * 128:(c + 1) * 128], bank[0:1, 0:128])

    with ExitStack() as ph:
        e0 = ph.enter_context
        soffs = e0(nc.sbuf_tensor("soffs", [E0, 1], F32))
        scnts = e0(nc.sbuf_tensor("scnts", [8, E0], F32))
        scm = e0(nc.sbuf_tensor("scm", [8, 1], F32))
        ohA = e0(nc.sbuf_tensor("ohA", [E0, T], F32))
        ohB = e0(nc.sbuf_tensor("ohB", [E0, T], F32))
        ohS = e0(nc.sbuf_tensor("ohS", [E0, T], F32))
        ohC = e0(nc.sbuf_tensor("ohC", [E0, T], F32))
        sxchd = [e0(nc.sbuf_tensor(f"sxch{i}", [128, 1024], BF16)) for i in range(2)]
        v.memset(sxchd[0][:, :], 0.0)
        for i0 in range(0, NSLOT + 1, 128):
            n = min(128, NSLOT + 1 - i0)
            s.dma_start(out=bkt[i0:i0 + n, :], in_=sxchd[0][0:n, :])
        s.dma_start(out=BO[NSLOT:NSLOT + 1, :], in_=sxchd[0][0:1, :])
        q.dma_start(out=scnts[:], in_=cnts[:, :])
        q.dma_start(out=scm[:], in_=coremask[:, :])
        t.matmul(P0[0:E0, 0:1], scnts[:, :], scm[:, :], start=True, stop=True)
        v.tensor_copy(soffs[:, :], P0[0:E0, 0:1])
        q.dma_start(out=A0, in_=rows_in[0:1, 0:T])
        q.dma_start(out=A1, in_=rows_in[0:1, T:2 * T])
        for (dst, src) in ((ohA, A0), (ohB, A1)):
            for hf in range(2):
                r.bcast(PST[0:E0, :], sones[0:1, 0:E0], src[0:1, hf * 1024:(hf + 1) * 1024])
                v.tensor_scalar(dst[:, hf * 1024:(hf + 1) * 1024], PST[0:E0, 0:1024],
                                siota16c[:, :], None, op0=ALU.is_equal)
        for (roff, oh, msk, wplane) in ((4, ohA, ohS, 2), (5, ohB, ohC, 3)):
            q.dma_start(out=A2, in_=rows_in[0:1, roff * T:(roff + 1) * T])
            for nn in range(4):
                bank = P0 if nn % 2 == 0 else P1
                t.matmul(bank[0:1, 0:512], soffs[:, :], oh[:, nn * 512:(nn + 1) * 512],
                         start=True, stop=True)
                v.tensor_copy(A3[0:1, nn * 512:(nn + 1) * 512], bank[0:1, 0:512])
            v.tensor_add(A2, A2, A3)
            v.tensor_single_scalar(A3, A2, float(C0), ALU.is_lt)
            q.dma_start(out=A2, in_=rows_in[0:1, (roff - 2) * T:(roff - 1) * T])
            v.tensor_mul(A2, A2, A3)
            row_to_col(A2, scol[:, :, wplane:wplane + 1])
            for hf in range(2):
                r.bcast(PST[0:E0, :], sones[0:1, 0:E0], A3[0:1, hf * 1024:(hf + 1) * 1024])
                v.tensor_mul(msk[:, hf * 1024:(hf + 1) * 1024],
                             oh[:, hf * 1024:(hf + 1) * 1024], PST[0:E0, 0:1024])
        v.tensor_add(ohA[:, :], ohS[:, :], ohC[:, :])
        v.memset(ohB[:, :], 0.0)
        v.tensor_tensor_scan(ohB[:, :], ohA[:, :], ohB[:, :], 0.0, ALU.add, ALU.add)
        v.tensor_sub(ohA[:, :], ohB[:, :], ohA[:, :])
        v.tensor_sub(ohB[:, :], ohB[:, :], ohC[:, :])
        v.tensor_mul(ohA[:, :], ohA[:, :], ohS[:, :])
        v.tensor_mul(ohB[:, :], ohB[:, :], ohC[:, :])
        for (kk_, ohm, sel, plane) in ((0, ohS, ohA, 0), (1, ohC, ohB, 1)):
            for nn in range(4):
                bank = P0 if nn % 2 == 0 else P1
                t.matmul(bank[0:1, 0:512], sonescf[0:E0, :], sel[:, nn * 512:(nn + 1) * 512],
                         start=True, stop=True)
                v.tensor_copy(A2[0:1, nn * 512:(nn + 1) * 512], bank[0:1, 0:512])
            for nn in range(4):
                bank = P0 if nn % 2 == 0 else P1
                t.matmul(bank[0:1, 0:512], sonescf[0:E0, :], ohm[:, nn * 512:(nn + 1) * 512],
                         start=True, stop=True)
                v.tensor_copy(A3[0:1, nn * 512:(nn + 1) * 512], bank[0:1, 0:512])
            v.tensor_single_scalar(A0, A2, float(CL), ALU.is_lt)
            v.tensor_mul(A3, A3, A0)
            q.dma_start(out=A0, in_=rows_in[0:1, kk_ * T:(kk_ + 1) * T])
            v.tensor_scalar_mul(A0, A0, float(CL))
            v.tensor_add(A0, A0, A2)
            v.tensor_scalar_sub(A0, A0, float(NSLOT))
            v.tensor_mul(A0, A0, A3)
            v.tensor_scalar_add(A0, A0, float(NSLOT))
            row_to_col(A0, scol[:, :, plane:plane + 1])
        for c in range(NC):
            xb = sxchd[c % 2]
            v.tensor_copy(sicol[:, c, :], scol[:, c, 0:2])
            q.dma_start(out=xb[:, :], in_=xnbf_tok[c * 128:(c + 1) * 128, :])
            g.indirect_dma_start(bkt[:, :], IndirectOffsetOnAxis(sicol[:, c, 0:1], 0), xb[:, :], None)
            g.indirect_dma_start(bkt[:, :], IndirectOffsetOnAxis(sicol[:, c, 1:2], 0), xb[:, :], None)
        q.dma_start(out=A0, in_=rows_in[0:1, 6 * T:7 * T])
        q.dma_start(out=A1, in_=rows_in[0:1, 7 * T:8 * T])
        v.tensor_mul(A2, A0, A1)
        v.tensor_scalar_mul(A2, A2, -1.0)
        row_to_col(A2, scol[:, :, 4:5])
        row_to_col(A1, scol[:, :, 5:6])

    r.fence()
    with ExitStack() as ph:
        e1 = ph.enter_context
        swg = [e1(nc.sbuf_tensor(f"swg{i}", [128, NT, F0], BF16)) for i in range(2)]
        swu = [e1(nc.sbuf_tensor(f"swu{i}", [128, NT, F0], BF16)) for i in range(2)]
        swd = [e1(nc.sbuf_tensor(f"swd{i}", [128, NF, D], BF16)) for i in range(2)]
        sbk = e1(nc.sbuf_tensor("sbk", [128, 2, 1024], BF16))
        sbkT = [e1(nc.sbuf_tensor(f"sbkT{i}", [128, NT, CL], BF16)) for i in range(2)]
        sh1 = [e1(nc.sbuf_tensor(f"sh1{i}", [128, NF, CL], BF16)) for i in range(2)]
        sbo = e1(nc.sbuf_tensor("sbo", [128, 1024], BF16))
        BK4 = [P0[:, 0:512], P1[:, 0:512], PST[:, 0:512], PST[:, 512:1024]]

        def bucket_load(ei, buf):
            q.dma_start(out=sbk[:, 0, :], in_=bkt[ei * CL:ei * CL + 128, :])
            q.dma_start(out=sbk[:, 1, :], in_=bkt[ei * CL + 128:ei * CL + 256, :])

        def transp_steps(ei, buf):
            for cc in range(4):
                n = 128 if cc < 3 else CL - 384
                bank = (PE2 if cc % 2 == 0 else PF)[:, :].bitcast(BF16)
                for j in range(NT):
                    t.transpose(bank[0:128, j * 128:j * 128 + n],
                                sbk[0:n, cc % 2, j * 128:(j + 1) * 128], sid128b[0:n, 0:n])
                for j in range(NT):
                    v.tensor_copy(sbkT[buf][:, j, cc * 128:cc * 128 + n],
                                  bank[0:128, j * 128:j * 128 + n])
                if cc == 1:
                    q.dma_start(out=sbk[:, 0, :], in_=bkt[ei * CL + 256:ei * CL + 384, :])
                    q.dma_start(out=sbk[0:CL - 384, 1, :], in_=bkt[ei * CL + 384:ei * CL + CL, :])
                yield

        def wload(ei, buf):
            g.dma_start(out=swg[buf][:], in_=wg_h[ei].rearrange("(a p) f -> p a f", p=128))
            g.dma_start(out=swu[buf][:], in_=wu_h[ei].rearrange("(a p) f -> p a f", p=128))
            g.dma_start(out=swd[buf][:], in_=wd_h[ei].rearrange("(a p) f -> p a f", p=128))

        rot = [0]

        def nbank():
            b = BK4[rot[0] % 4]
            rot[0] += 1
            return b

        def down_groups(ei):
            hb = ei % 2
            wdb = swd[hb]
            for sc in range(4):
                n = 128 if sc < 3 else CL - 384
                for nn in range(2):
                    bank = nbank()
                    for kk in range(NF):
                        t.matmul(bank[0:n, :], sh1[hb][:, kk, sc * 128:sc * 128 + n],
                                 wdb[:, kk, nn * 512:(nn + 1) * 512],
                                 start=(kk == 0), stop=(kk == NF - 1))
                    v.tensor_copy(sbo[0:n, nn * 512:(nn + 1) * 512], bank[0:n, :])
                    if nn == 1:
                        q.dma_start(out=BO[ei * CL + sc * 128:ei * CL + sc * 128 + n, :],
                                    in_=sbo[0:n, :])
                    yield

        bucket_load(0, 0)
        wload(0, 0)
        for _ in transp_steps(0, 0):
            pass
        dn = iter(())
        for ei in range(E0):
            buf = ei % 2
            wgb, wub, wdb = swg[buf], swu[buf], swd[buf]
            if ei + 1 < E0:
                bucket_load(ei + 1, 1 - buf)
                nxt = transp_steps(ei + 1, 1 - buf)
            else:
                nxt = iter(())
            for m in range(NF):
                bank = nbank()
                for kk in range(NT):
                    t.matmul(bank[:, 0:CL], wgb[:, kk, m * 128:(m + 1) * 128], sbkT[buf][:, kk, :],
                             start=(kk == 0), stop=(kk == NT - 1))
                s.activation(sh1[buf][:, m, :], bank[:, 0:CL], AF.Silu)
                next(dn, None)
            for _ in dn:
                pass
            if ei + 1 < E0:
                wload(ei + 1, 1 - buf)
            for m in range(NF):
                bank = nbank()
                for kk in range(NT):
                    t.matmul(bank[:, 0:CL], wub[:, kk, m * 128:(m + 1) * 128], sbkT[buf][:, kk, :],
                             start=(kk == 0), stop=(kk == NT - 1))
                v.tensor_mul(sh1[buf][:, m, :], sh1[buf][:, m, :], bank[:, 0:CL])
                if m % 2 == 1:
                    next(nxt, None)
            for _ in nxt:
                pass
            dn = down_groups(ei)
        for _ in dn:
            pass

    r.fence()
    sxhT = e(nc.sbuf_tensor("sxhT", [128, NT, T], BF16))
    sxlT = e(nc.sbuf_tensor("sxlT", [128, NT, T], BF16))
    with ExitStack() as ph:
        e2 = ph.enter_context
        sG0 = [e2(nc.sbuf_tensor(f"sG0_{i}", [128, 1024], BF16)) for i in range(2)]
        sG1 = [e2(nc.sbuf_tensor(f"sG1_{i}", [128, 1024], BF16)) for i in range(2)]
        sxc = [e2(nc.sbuf_tensor(f"sxc{i}", [128, 1024], F32)) for i in range(2)]
        soc = [e2(nc.sbuf_tensor(f"soc{i}", [128, 1024], BF16)) for i in range(2)]
        sshc = [e2(nc.sbuf_tensor(f"sshc{i}", [128, 1024], BF16)) for i in range(2)]
        sx1 = e2(nc.sbuf_tensor("sx1", [128, 1024], F32))
        sx2 = [e2(nc.sbuf_tensor(f"sx2_{i}", [128, 1024], F32)) for i in range(2)]
        srt = e2(nc.sbuf_tensor("srt", [128, 1024], F32))
        sgg = e2(nc.sbuf_tensor("sgg", [128, 1024], F32))
        shi = e2(nc.sbuf_tensor("shi", [128, 1024], BF16))
        slo = e2(nc.sbuf_tensor("slo", [128, 1024], BF16))
        ssq = e2(nc.sbuf_tensor("ssq", [128, 1024], BF16))
        scw = e2(nc.sbuf_tensor("scw", [128, NC, 4], F32))

        def fetch1(c):
            i = c % 2
            g.indirect_dma_start(sG0[i][:, :], None, BO[:, :], IndirectOffsetOnAxis(sicol[:, c, 0:1], 0))
            g.indirect_dma_start(sG1[i][:, :], None, BO[:, :], IndirectOffsetOnAxis(sicol[:, c, 1:2], 0))
            q.dma_start(out=sxc[i][:, :], in_=x_tok[c * 128:(c + 1) * 128, :])
            q.dma_start(out=soc[i][:, :], in_=o_tok[c * 128:(c + 1) * 128, :])
            q.dma_start(out=sshc[i][:, :], in_=sh_tok[c * 128:(c + 1) * 128, :])

        fetch1(0)
        for c in range(NC):
            i = c % 2
            if c + 1 < NC:
                fetch1(c + 1)
            v.tensor_scalar(srt[:, :], sG0[i][:, :], scol[:, c, 2:3], None, op0=ALU.mult)
            v.scalar_tensor_tensor(srt[:, :], sG1[i][:, :], scol[:, c, 3:4], srt[:, :], ALU.mult, ALU.add)
            v.tensor_scalar(sgg[:, :], sxc[i][:, :], scol[:, c, 5:6], scol[:, c, 4:5], op0=ALU.mult, op1=ALU.add)
            v.tensor_add(srt[:, :], srt[:, :], sgg[:, :])
            v.tensor_tensor(srt[:, :], srt[:, :], sshc[i][:, :], ALU.add)
            s.activation(srt[:, :], srt[:, :], AF.Silu)
            v.tensor_tensor(sgg[:, :], sgg[:, :], soc[i][:, :], ALU.add)
            v.tensor_mul(sgg[:, :], sgg[:, :], srt[:, :])
            v.tensor_add(sx1[:, :], sxc[i][:, :], sgg[:, :])
            q.dma_start(out=x1_o[c * 128:(c + 1) * 128, :], in_=sx1[:, :])
            v.reduce_sum(scw[:, c, 0:1], sx1[:, :], axis=AX.X)
            v.tensor_mul(ssq[:, :], sx1[:, :], sx1[:, :])
            v.reduce_sum(scw[:, c, 1:2], ssq[:, :], axis=AX.X)
            v.tensor_scalar_mul(scw[:, c, 0:1], scw[:, c, 0:1], 1.0 / D)
            v.tensor_scalar_mul(scw[:, c, 1:2], scw[:, c, 1:2], 1.0 / D)
            v.tensor_mul(scw[:, c, 2:3], scw[:, c, 0:1], scw[:, c, 0:1])
            v.tensor_sub(scw[:, c, 1:2], scw[:, c, 1:2], scw[:, c, 2:3])
            v.tensor_scalar_add(scw[:, c, 1:2], scw[:, c, 1:2], EPS)
        s.activation(scw[:, :, 2:3], scw[:, :, 1:2], AF.Sqrt)
        v.reciprocal(scw[:, :, 2:3], scw[:, :, 2:3])
        v.tensor_mul(scw[:, :, 3:4], scw[:, :, 0:1], scw[:, :, 2:3])
        v.tensor_scalar_mul(scw[:, :, 3:4], scw[:, :, 3:4], -1.0)
        g.dma_start(out=sx2[0][:, :], in_=x1_o[0:128, :])
        for c in range(NC):
            i = c % 2
            if c + 1 < NC:
                g.dma_start(out=sx2[1 - i][:, :], in_=x1_o[(c + 1) * 128:(c + 2) * 128, :])
            v.tensor_scalar(sgg[:, :], sx2[i][:, :], scw[:, c, 2:3], scw[:, c, 3:4],
                            op0=ALU.mult, op1=ALU.add)
            v.tensor_copy(shi[:, :], sgg[:, :])
            v.tensor_tensor(srt[:, :], sgg[:, :], shi[:, :], ALU.subtract)
            v.tensor_copy(slo[:, :], srt[:, :])
            for (src, dst) in ((shi, sxhT), (slo, sxlT)):
                for j in range(NT):
                    bank = (PE2 if j % 2 == 0 else PF)[:, :].bitcast(BF16)
                    t.transpose(bank[0:128, 0:128], src[:, j * 128:(j + 1) * 128], sid128b[:, :])
                    v.tensor_copy(dst[:, j, c * 128:(c + 1) * 128], bank[0:128, 0:128])

    r.fence()
    with ExitStack() as ph:
        e3 = ph.enter_context
        stmp0 = e3(nc.sbuf_tensor("stmp0", [128, 1024], F32))
        sWg = e3(nc.sbuf_tensor("sWg", [128, NT, E1], F32))
        sWgh = e3(nc.sbuf_tensor("sWgh", [128, NT, E1], BF16))
        sWgl = e3(nc.sbuf_tensor("sWgl", [128, NT, E1], BF16))
        slog8 = e3(nc.sbuf_tensor("slog8", [E1, T], F32))
        sltm = e3(nc.sbuf_tensor("sltm", [128, NC, E1], F32))
        sw8 = e3(nc.sbuf_tensor("sw8", [128, NC, 12], F32))
        stk8 = e3(nc.sbuf_tensor("stk8", [128, 24], F32))
        szer8 = e3(nc.sbuf_tensor("szer8", [128, E1], F32))
        ohA1 = e3(nc.sbuf_tensor("ohA1", [E1, T], F32))
        ohC1 = e3(nc.sbuf_tensor("ohC1", [E1, T], F32))
        q.dma_start(out=sWg[:], in_=Wg1.rearrange("(a p) f -> p a f", p=128))
        v.tensor_copy(sWgh[:], sWg[:])
        v.tensor_tensor(sWg[:], sWg[:], sWgh[:], ALU.subtract)
        v.tensor_copy(sWgl[:], sWg[:])
        v.memset(szer8[:], 0.0)
        for nn in range(4):
            bank = P0 if nn % 2 == 0 else P1
            first = True
            for (xs, ws) in ((sxhT, sWgh), (sxhT, sWgl), (sxlT, sWgh)):
                for kk in range(NT):
                    t.matmul(bank[0:E1, 0:512], ws[:, kk, :], xs[:, kk, nn * 512:(nn + 1) * 512],
                             start=first, stop=(not first and kk == NT - 1 and ws is sWgh and xs is sxlT))
                    first = False
            v.tensor_copy(slog8[:, nn * 512:(nn + 1) * 512], bank[0:E1, 0:512])
        for c in range(NC):
            bank = P0 if c % 2 == 0 else P1
            t.transpose(bank[0:128, 0:E1], slog8[:, c * 128:(c + 1) * 128], sident8[:, :])
            v.tensor_copy(sltm[:, c, :], bank[0:128, 0:E1])
        for c in range(NC):
            ch = sltm[:, c, :]
            w8 = sw8[:, c, :]
            v.max(out=w8[:, 0:8], in_=ch)
            v.tensor_scalar_mul(w8[:, 8:9], w8[:, 0:1], -1.0)
            s.activation(stk8[:, 0:E1], ch, AF.Exp, bias=w8[:, 8:9], accum_out=w8[:, 9:10])
            v.reciprocal(scol[:, c, 6:7], w8[:, 9:10])
            v.tensor_single_scalar(stk8[:, 0:E1], ch, w8[:, 0:1], ALU.is_ge)
            v.tensor_tensor_scan(stk8[:, 8:16], stk8[:, 0:E1], szer8[:, 0:E1], 0.0, ALU.add, ALU.add)
            v.tensor_single_scalar(stk8[:, 8:16], stk8[:, 8:16], 1.0, ALU.is_equal)
            v.tensor_mul(stk8[:, 0:E1], stk8[:, 0:E1], stk8[:, 8:16])
            v.tensor_mul(stk8[:, 8:16], stk8[:, 0:E1], siota8r[:, :])
            v.reduce_sum(scol[:, c, 7:8], stk8[:, 8:16], axis=AX.X)
        col_to_row(scol[:, :, 7:8], A0)
        q.dma_start(out=rows1_o[0:1, 0:T], in_=A0)
        for hf in range(2):
            r.bcast(PST[0:E1, :], sones[0:1, 0:E1], A0[0:1, hf * 1024:(hf + 1) * 1024])
            v.tensor_scalar(ohA1[:, hf * 1024:(hf + 1) * 1024], PST[0:E1, 0:1024],
                            siota8c[:, :], None, op0=ALU.is_equal)
        v.memset(ohC1[:, :], 0.0)
        v.tensor_tensor_scan(ohC1[:, :], ohA1[:, :], ohC1[:, :], 0.0, ALU.add, ALU.add)
        v.tensor_copy(scnt1[:, :], ohC1[:, T - 1:T])
        q.dma_start(out=cnt1_o[:, :], in_=scnt1[:, :])
        v.tensor_sub(ohC1[:, :], ohC1[:, :], ohA1[:, :])
        v.tensor_mul(ohC1[:, :], ohC1[:, :], ohA1[:, :])
        for nn in range(4):
            bank = P0 if nn % 2 == 0 else P1
            t.matmul(bank[0:1, 0:512], sonescf[0:E1, :], ohC1[:, nn * 512:(nn + 1) * 512],
                     start=True, stop=True)
            v.tensor_copy(A1[0:1, nn * 512:(nn + 1) * 512], bank[0:1, 0:512])
        q.dma_start(out=rows1_o[0:1, T:2 * T], in_=A1)
        sq_a = e3(nc.sbuf_tensor("sq_a", [HD, T], BF16))
        sq_b = e3(nc.sbuf_tensor("sq_b", [HD + 1, T], BF16))
        sk_a = e3(nc.sbuf_tensor("sk_a", [HD, T], BF16))
        sk_b = e3(nc.sbuf_tensor("sk_b", [HD + 1, T], BF16))
        svt1 = e3(nc.sbuf_tensor("svt1", [128, NC, 128], BF16))
        sst1 = e3(nc.sbuf_tensor("sst1", [128, T], BF16))
        sE1b = e3(nc.sbuf_tensor("sE1b", [128, T], BF16))
        swq = e3(nc.sbuf_tensor("swq", [128, NT, 128], BF16))
        shdm = e3(nc.sbuf_tensor("shdm", [128, 1], F32))
        shdmi = e3(nc.sbuf_tensor("shdmi", [128, 1], I32))
        q.dma_start(out=shdmi[:], in_=hdm[:, :])
        v.tensor_copy(shdm[:, :], shdmi[:, :])
        for (W, da, db) in ((wq1, sq_a, sq_b), (wk1, sk_a, sk_b)):
            g.dma_start(out=swq[:], in_=W.rearrange("(a p) f -> p a f", p=128))
            for nn in range(4):
                bank = P0 if nn % 2 == 0 else P1
                for kk in range(NT):
                    t.matmul(bank[0:HD, 0:512], swq[:, kk, 0:HD],
                             sxhT[:, kk, nn * 512:(nn + 1) * 512],
                             start=(kk == 0), stop=(kk == NT - 1))
                v.tensor_copy(da[:, nn * 512:(nn + 1) * 512], bank[0:HD, 0:512])
                bank2 = P1 if nn % 2 == 0 else P0
                for kk in range(NT):
                    t.matmul(bank2[0:HD, 0:512], swq[:, kk, HD:128],
                             sxhT[:, kk, nn * 512:(nn + 1) * 512],
                             start=(kk == 0), stop=(kk == NT - 1))
                v.tensor_copy(db[0:HD, nn * 512:(nn + 1) * 512], bank2[0:HD, 0:512])
        v.memset(sk_b[HD:HD + 1, :], 1.0)
        g.dma_start(out=swq[:], in_=wv1.rearrange("(a p) f -> p a f", p=128))
        for c in range(NC):
            bank = P0 if c % 2 == 0 else P1
            for kk in range(NT):
                t.matmul(bank[:, 0:128], sxhT[:, kk, c * 128:(c + 1) * 128], swq[:, kk, 0:128],
                         start=(kk == 0), stop=(kk == NT - 1))
            v.tensor_copy(svt1[:, c, :], bank[:, 0:128])
        for qc in range(NC):
            for nn in range(4):
                bank = PE2 if nn % 2 == 0 else PF
                t.matmul(bank[:, 0:512], sq_a[:, qc * 128:(qc + 1) * 128],
                         sk_a[:, nn * 512:(nn + 1) * 512], start=True, stop=False)
                t.matmul(bank[:, 0:512], sq_b[0:HD, qc * 128:(qc + 1) * 128],
                         sk_b[0:HD, nn * 512:(nn + 1) * 512], start=False, stop=True)
                v.reduce_max(scol[:, qc, 7:8] if nn == 0 else stmp0[:, nn:nn + 1],
                             bank[:, 0:512], axis=AX.X)
                if nn > 0:
                    v.tensor_max(scol[:, qc, 7:8], scol[:, qc, 7:8], stmp0[:, nn:nn + 1])
        col_to_row(scol[:, :, 7:8], A2)
        v.tensor_scalar_mul(A2, A2, -1.0)
        g.dma_start(out=sq_b[HD:HD + 1, :], in_=A2)
        SC1 = 1.0 / float(np.sqrt(128.0))
        for hf in range(2):
            qs = slice(hf * 1024, (hf + 1) * 1024)
            def sim1_mm(kc):
                bank = PE2 if kc % 2 == 0 else PF
                for nn in range(2):
                    t.matmul(bank[:, nn * 512:(nn + 1) * 512],
                             sk_a[:, kc * 128:(kc + 1) * 128],
                             sq_a[:, hf * 1024 + nn * 512:hf * 1024 + (nn + 1) * 512],
                             start=True, stop=False)
                    t.matmul(bank[:, nn * 512:(nn + 1) * 512],
                             sk_b[:, kc * 128:(kc + 1) * 128],
                             sq_b[:, hf * 1024 + nn * 512:hf * 1024 + (nn + 1) * 512],
                             start=False, stop=True)
            sim1_mm(0)
            for kc in range(NC):
                bank = PE2 if kc % 2 == 0 else PF
                eb = sE1b[:, 0:1024] if kc % 2 == 0 else sE1b[:, 1024:2048]
                s.activation(eb, bank[:, 0:1024], AF.Exp, scale=SC1)
                if kc + 1 < NC:
                    sim1_mm(kc + 1)
                for nn in range(2):
                    t.matmul(PST[:, nn * 512:(nn + 1) * 512], svt1[:, kc, :],
                             eb[:, nn * 512:(nn + 1) * 512],
                             start=(kc == 0), stop=(kc == NC - 1))
                t.matmul(P0[0:1, 0:512], sonesbc[:, :], eb[:, 0:512],
                         start=(kc == 0), stop=(kc == NC - 1))
                t.matmul(P1[0:1, 0:512], sonesbc[:, :], eb[:, 512:1024],
                         start=(kc == 0), stop=(kc == NC - 1))
            v.tensor_copy(A3[0:1, hf * 1024:hf * 1024 + 512], P0[0:1, 0:512])
            v.tensor_copy(A3[0:1, hf * 1024 + 512:(hf + 1) * 1024], P1[0:1, 0:512])
            v.reciprocal(A3[0:1, qs], A3[0:1, qs])
            v.tensor_copy(stmp0[:, 0:1024], PST[:, 0:1024])
            r.bcast(PE2[:, :], sones[:, :], A3[0:1, qs])
            v.tensor_mul(stmp0[:, 0:1024], stmp0[:, 0:1024], PE2[:, 0:1024])
            v.tensor_scalar(sst1[:, qs], stmp0[:, 0:1024], shdm[:, 0:1], None, op0=ALU.mult)
        swo = e3(nc.sbuf_tensor("swo", [128, D], BF16))
        sstg1 = e3(nc.sbuf_tensor("sstg1", [128, 1024], BF16))
        g.dma_start(out=swo[:], in_=wo1[:, :])
        for c in range(NC):
            for nn in range(2):
                bank = P0 if nn % 2 == 0 else P1
                t.matmul(bank[:, :], sst1[:, c * 128:(c + 1) * 128],
                         swo[:, nn * 512:(nn + 1) * 512], start=True, stop=True)
                v.tensor_copy(sstg1[:, nn * 512:(nn + 1) * 512], bank[:, :])
            q.dma_start(out=s1_o[c * 128:(c + 1) * 128, :], in_=sstg1[:, :])
        sWe = e3(nc.sbuf_tensor("sWe", [128, NT, D], BF16))
        sbeb = e3(nc.sbuf_tensor("sbeb", [128, D], F32))
        g.dma_start(out=sWe[:], in_=We1.rearrange("(a p) f -> p a f", p=128))
        q.dma_start(out=sbeb[:], in_=be_b[:, :])
        for c in range(NC):
            for nn in range(2):
                bank = BK6[(c * 2 + nn) % 6]
                for kk in range(NT):
                    t.matmul(bank[:, :], sxhT[:, kk, c * 128:(c + 1) * 128],
                             sWe[:, kk, nn * 512:(nn + 1) * 512],
                             start=(kk == 0), stop=(kk == NT - 1))
                v.tensor_tensor(stmp0[:, nn * 512:(nn + 1) * 512], bank[:, :],
                                sbeb[:, nn * 512:(nn + 1) * 512], ALU.add)
            v.tensor_scalar(sstg1[:, :], stmp0[:, 0:1024], scol[:, c, 6:7], None, op0=ALU.mult)
            q.dma_start(out=z_o[c * 128:(c + 1) * 128, :], in_=sstg1[:, :])
    ctx.close()
    ectx = ExitStack()
    with ectx:
        r.emit(ectx)
    return nc



def build_C():
    nc = bass.Bass("TRN2", target_bir_lowering=False, detect_race_conditions=False)
    dtr = nc.dram_tensor
    I = lambda n, s, d=F32: dtr(n, s, d, kind="ExternalInput").ap()
    O = lambda n, s, d=F32: dtr(n, s, d, kind="ExternalOutput").ap()
    x1 = I("x1", [T, D])
    z_tok = I("z_tok", [T, D], BF16)
    s1_tok = I("s1_tok", [T, D], BF16)
    rows1 = I("rows1", [1, 2 * T])
    cnt1s = I("cnt1s", [8, E1])
    coremask = I("coremask", [8, 1])
    onesr = I("onesr", [1, 128])
    iota8c = I("iota8c", [E1, 1])
    id128 = I("id128", [128, 128])
    y_o = O("y", [T, D])

    r = Rec(nc)
    q, g, v, s, t = r.q, r.g, r.v, r.s, r.t
    ctx = ExitStack()
    e = ctx.enter_context
    ar = e(nc.sbuf_tensor("arena", [1, 4 * T], F32))
    A0 = ar[0:1, 0:T]; A1 = ar[0:1, T:2 * T]; A2 = ar[0:1, 2 * T:3 * T]; A3 = ar[0:1, 3 * T:4 * T]
    sones = e(nc.sbuf_tensor("sones", [1, 128], F32))
    siota8c = e(nc.sbuf_tensor("siota8c", [E1, 1], F32))
    sid128 = e(nc.sbuf_tensor("sid128", [128, 128], F32))
    scm = e(nc.sbuf_tensor("scm", [8, 1], F32))
    scnts = e(nc.sbuf_tensor("scnts", [8, E1], F32))
    soffs = e(nc.sbuf_tensor("soffs", [E1, 1], F32))
    swin = e(nc.sbuf_tensor("swin", [128, NC, 1], F32))
    ohA1 = e(nc.sbuf_tensor("ohA1", [E1, T], F32))
    sz = [e(nc.sbuf_tensor(f"sz{i}", [128, 1024], BF16)) for i in range(2)]
    ss1 = [e(nc.sbuf_tensor(f"ss1{i}", [128, 1024], BF16)) for i in range(2)]
    sx1 = [e(nc.sbuf_tensor(f"sx1{i}", [128, 1024], F32)) for i in range(2)]
    sg1 = [e(nc.sbuf_tensor(f"sg1{i}", [128, 1024], F32)) for i in range(2)]
    P0 = e(nc.psum_tensor("P0", [128, 512], F32))
    P1 = e(nc.psum_tensor("P1", [128, 512], F32))
    PST = e(nc.psum_tensor("PST", [128, 1024], F32))

    q.dma_start(out=sones[:], in_=onesr[:, :])
    q.dma_start(out=siota8c[:], in_=iota8c[:, :])
    q.dma_start(out=sid128[:], in_=id128[:, :])
    q.dma_start(out=scm[:], in_=coremask[:, :])
    q.dma_start(out=scnts[:], in_=cnt1s[:, :])
    t.matmul(P0[0:E1, 0:1], scnts[:, :], scm[:, :], start=True, stop=True)
    v.tensor_copy(soffs[:, :], P0[0:E1, 0:1])
    q.dma_start(out=A0, in_=rows1[0:1, 0:T])
    for hf in range(2):
        r.bcast(PST[0:E1, :], sones[0:1, 0:E1], A0[0:1, hf * 1024:(hf + 1) * 1024])
        v.tensor_scalar(ohA1[:, hf * 1024:(hf + 1) * 1024], PST[0:E1, 0:1024],
                        siota8c[:, :], None, op0=ALU.is_equal)
    for nn in range(4):
        bank = P0 if nn % 2 == 0 else P1
        t.matmul(bank[0:1, 0:512], soffs[:, :], ohA1[:, nn * 512:(nn + 1) * 512],
                 start=True, stop=True)
        v.tensor_copy(A2[0:1, nn * 512:(nn + 1) * 512], bank[0:1, 0:512])
    q.dma_start(out=A1, in_=rows1[0:1, T:2 * T])
    v.tensor_add(A1, A1, A2)
    v.tensor_single_scalar(A1, A1, float(C1), ALU.is_lt)
    for c in range(NC):
        bank = P0 if c % 2 == 0 else P1
        t.transpose(bank[0:128, 0:1], A1[0:1, c * 128:(c + 1) * 128], sid128[0:1, 0:1])
        v.tensor_copy(swin[:, c, :], bank[0:128, 0:1])
    def cfetch(c):
        i = c % 2
        q.dma_start(out=sz[i][:, :], in_=z_tok[c * 128:(c + 1) * 128, :])
        g.dma_start(out=ss1[i][:, :], in_=s1_tok[c * 128:(c + 1) * 128, :])
        q.dma_start(out=sx1[i][:, :], in_=x1[c * 128:(c + 1) * 128, :])

    cfetch(0)
    for c in range(NC):
        i = c % 2
        if c + 1 < NC:
            cfetch(c + 1)
        s.activation(sg1[i][:, :], sz[i][:, :], AF.Silu, scale=swin[:, c, 0:1])
        v.tensor_tensor(sg1[i][:, :], sg1[i][:, :], ss1[i][:, :], ALU.mult)
        v.tensor_add(sg1[i][:, :], sg1[i][:, :], sx1[i][:, :])
        g.dma_start(out=y_o[c * 128:(c + 1) * 128, :], in_=sg1[i][:, :])
    ctx.close()
    ectx = ExitStack()
    with ectx:
        r.emit(ectx)
    return nc



def _consts16():
    import numpy as _np
    return {
        "onesr": _np.ones((1, 128), _np.float32),
        "onescf": _np.ones((128, 1), _np.float32),
        "onesbc": _np.ones((128, 1), ml_dtypes.bfloat16),
        "iota16c": _np.arange(E0, dtype=_np.float32).reshape(E0, 1),
        "iota16r": _np.tile(_np.arange(E0, dtype=_np.float32), (128, 1)),
        "ident16": _np.eye(E0, dtype=_np.float32),
        "id128": _np.eye(128, dtype=_np.float32),
    }


def prep_A(x_b, ins):
    Wo = ins["L0_attn_Wo"]
    WoR = Wo.reshape(NH, HD, D).transpose(1, 0, 2).reshape(HD, NH * D)
    im = {
        "xT": np.ascontiguousarray(x_b.T).astype(np.float32),
        "Wq": ins["L0_attn_Wq"], "Wk": ins["L0_attn_Wk"], "Wv": ins["L0_attn_Wv"],
        "WoR": np.ascontiguousarray(WoR),
        "bqR": np.ascontiguousarray(ins["L0_attn_bq"].reshape(NH, HD).T),
        "bkR": np.ascontiguousarray(ins["L0_attn_bk"].reshape(NH, HD).T),
        "bvB": np.ascontiguousarray(np.broadcast_to(ins["L0_attn_bv"], (128, D))),
        "bo": ins["L0_attn_bo"].reshape(D, 1),
        "lnw": ins["L0_attn_ln_w"].reshape(D, 1),
        "lnb": ins["L0_attn_ln_b"].reshape(D, 1),
        "rw": ins["L0_ffn_router_w"], "rb": ins["L0_ffn_router_b"].reshape(E0, 1),
        "swg": ins["L0_ffn_sw_gate"], "swu": ins["L0_ffn_sw_up"], "swd": ins["L0_ffn_sw_down"],
    }
    im.update(_consts16())
    return im


def prep_B(b, x_b, ra, cnts_all, ins):
    im = {
        "x_tok": np.ascontiguousarray(x_b).astype(np.float32),
        "o_tok": np.ascontiguousarray(np.asarray(ra["oT"]).T),
        "sh_tok": np.ascontiguousarray(np.asarray(ra["shT"]).T),
        "xnbf_tok": np.ascontiguousarray(np.asarray(ra["xnbf"]).T),
        "rows_in": np.asarray(ra["rows"]),
        "cnts": cnts_all,
        "coremask": (np.arange(8) < b).astype(np.float32).reshape(8, 1),
        "wg_h": ins["L0_ffn_w_gate"], "wu_h": ins["L0_ffn_w_up"], "wd_h": ins["L0_ffn_w_down"],
        "wq1": ins["L1_attn_w_q"], "wk1": ins["L1_attn_w_k"], "wv1": ins["L1_attn_w_v"],
        "wo1": ins["L1_attn_w_o"],
        "Wg1": ins["L1_ffn_Wg"], "We1": ins["L1_ffn_We"],
        "be_b": np.ascontiguousarray(np.broadcast_to(ins["L1_ffn_be"], (128, D))).astype(np.float32),
        "hdm": ins["L1_attn_head_dim_mask"].reshape(128, 1).astype(np.int32),
        "iota8c": np.arange(E1, dtype=np.float32).reshape(E1, 1),
        "iota8r": np.tile(np.arange(E1, dtype=np.float32), (128, 1)),
        "ident8": np.eye(E1, dtype=np.float32),
        "id128b": np.eye(128, dtype=ml_dtypes.bfloat16),
    }
    im.update({k: v for k, v in _consts16().items() if k in ("onesr", "onescf", "onesbc", "id128", "iota16c")})
    return im


def prep_C(b, rb_, cnt1_all):
    return {
        "x1": np.asarray(rb_["x1"]),
        "z_tok": np.asarray(rb_["z_tok"]),
        "s1_tok": np.asarray(rb_["s1_tok"]),
        "rows1": np.asarray(rb_["rows1"]),
        "cnt1s": cnt1_all,
        "coremask": (np.arange(8) < b).astype(np.float32).reshape(8, 1),
        "onesr": np.ones((1, 128), np.float32),
        "iota8c": np.arange(E1, dtype=np.float32).reshape(E1, 1),
        "id128": np.eye(128, dtype=np.float32),
    }


def kernel(**inputs):
    ins = {k: np.asarray(v) for k, v in inputs.items()}
    x = ins["L0_attn_hidden_states"]
    ncA = build_A()
    imsA = [prep_A(x[b], ins) for b in range(8)]
    resA = run_bass_kernel_spmd(ncA, imsA, core_ids=list(range(8))).results
    cnts_all = np.stack([np.asarray(resA[b]["cnt"])[:, 0] for b in range(8)]).astype(np.float32)
    ncB = build_B()
    imsB = [prep_B(b, x[b], resA[b], cnts_all, ins) for b in range(8)]
    resB = run_bass_kernel_spmd(ncB, imsB, core_ids=list(range(8))).results
    cnt1_all = np.stack([np.asarray(resB[b]["cnt1"])[:, 0] for b in range(8)]).astype(np.float32)
    ncC = build_C()
    imsC = [prep_C(b, resB[b], cnt1_all) for b in range(8)]
    resC = run_bass_kernel_spmd(ncC, imsC, core_ids=list(range(8))).results
    y = np.stack([np.asarray(resC[b]["y"]) for b in range(8)])
    return y.astype(np.float32)


if __name__ == "__main__":
    nc = build_A()
    print("built A ok")
    nc = build_B()
    print("built B ok")
